# version 15
# baseline (speedup 1.0000x reference)
import math

import numpy as np
import ml_dtypes

import concourse.bass as bass
import concourse.mybir as mybir
from concourse.bass_utils import run_bass_kernel_spmd
from concourse.expressions import RuntimeValue
from concourse.bass_types import DynSlice

F32 = mybir.dt.float32
BF16 = mybir.dt.bfloat16
AF = mybir.ActivationFunctionType
ALU = mybir.AluOpType
BF = ml_dtypes.bfloat16
AXX = mybir.AxisListType.X

D = 2048
DC = 16
NH = 16
NKV = 4
LORA = 512
CC = 4
DN = 128
DR = 64
DQK = 192
DV = 128
HID = 8192
HC = 64
OWN = 2048
SCALE = float(np.float32(1.0 / math.sqrt(DQK)))
EPS = 1e-5
BIGAUG = 1.0e5
FWHT_SCALE = float(np.float32(1.0 / math.sqrt(HID)))

TWO_PI = 6.283185307179586
CW1 = float(np.float32(6.28125))
CW2 = float(np.float32(TWO_PI - 6.28125))
CW3 = float(TWO_PI - CW1 - float(np.float32(TWO_PI - 6.28125)))
MAGIC = float(np.float32(1.5 * 2 ** 23))
INVF_SCALE = float(np.float32(-math.log(10000.0) / 32.0))

H128NP = np.array([[(-1) ** bin(i & j).count("1") for j in range(128)]
                   for i in range(128)], np.float32)

ENGS = ("sync", "pool", "pe", "act", "dve")


class Cfg:
    def __init__(self, S):
        self.S = S
        self.NSLAB = S // 512
        self.NCH = S // 128
        self.SPC = self.NSLAB // 4
        self.FAKE = self.NCH
        self.L = [4 * self.SPC * (e + 1) for e in range(4)]


def slabs_of(core, cfg):
    r = core % 4
    if cfg.SPC > 1:
        return [cfg.SPC * e + ((r + e) % cfg.SPC) for e in range(4)]
    return [e for e in range(4)]


def qkv_perm():
    idx = []
    for h in range(NH):
        idx += [h * DQK + d for d in range(DN)]
    for p in range(NH // 2):
        he, ho = 2 * p, 2 * p + 1
        idx += [he * DQK + DN + d for d in range(32)]
        idx += [he * DQK + DN + 32 + d for d in range(32)]
        idx += [ho * DQK + DN + d for d in range(32)]
        idx += [ho * DQK + DN + 32 + d for d in range(32)]
    idx += list(range(NH * DQK, NH * DQK + LORA))
    return np.array(idx)


def kup_perm():
    idx = []
    for g in range(NKV):
        idx += [g * DQK + d for d in range(DN)]
    for p in range(NKV // 2):
        ge, go = 2 * p, 2 * p + 1
        idx += [ge * DQK + DN + d for d in range(32)]
        idx += [ge * DQK + DN + 32 + d for d in range(32)]
        idx += [go * DQK + DN + d for d in range(32)]
        idx += [go * DQK + DN + 32 + d for d in range(32)]
    return np.array(idx)



class P:

    def __init__(self, nc):
        self.nc = nc
        self.tick = {e: nc.alloc_semaphore("tk_" + e) for e in ENGS}
        self.dsem = {}

    def start(self, cur, eng):
        self.cur = cur
        self.eng = eng
        self.cnt = {e: 0 for e in ENGS}
        self.dcnt = {}
        self.tok = {}
        self.wm = {}

    def _sem(self, name):
        if name not in self.dsem:
            self.dsem[name] = self.nc.alloc_semaphore("d_" + name)
        return self.dsem[name]

    def after(self, keys):
        best = {}
        for k in keys:
            if k is None:
                continue
            kind, name, val = self.tok[k]
            wmk = (kind, name)
            if val > best.get(wmk, 0):
                best[wmk] = val
        for (kind, name), val in best.items():
            if self.wm.get((kind, name), 0) >= val:
                continue
            self.wm[(kind, name)] = val
            h = self.tick[name] if kind == "t" else self._sem(name)
            self.eng.wait_ge(h, val)

    AUTOCHAIN = ("dve", "pool", "act")

    def op(self, engname, fn, key=None, after=()):
        chain = engname in self.AUTOCHAIN
        if self.cur == engname:
            if chain and self.cnt[engname] > 0:
                k = ("t", engname)
                v = self.cnt[engname]
                if self.wm.get(k, 0) < v:
                    self.wm[k] = v
                    self.eng.wait_ge(self.tick[engname], v)
            self.after(after)
            inst = fn(self.eng)
        else:
            inst = None
        if key is not None or chain:
            self.cnt[engname] += 1
            if inst is not None:
                inst.then_inc(self.tick[engname], 1)
            if key is not None:
                self.tok[key] = ("t", engname, self.cnt[engname])

    def mm(self, fn, key=None, after=()):
        self.op("pe", fn, key, after)

    def act(self, fn, key=None, after=()):
        self.op("act", fn, key, after)

    def dve(self, fn, key=None, after=()):
        self.op("dve", fn, key, after)

    def pool(self, fn, key=None, after=()):
        self.op("pool", fn, key, after)

    def raw(self, engname, fn):
        if self.cur == engname:
            fn(self.eng)

    def dma(self, queue, semname, fn, key=None, after=()):
        self._sem(semname)
        if self.cur == queue:
            self.after(after)
            inst = fn(self.eng)
            inst.then_inc(self.dsem[semname], 16)
        self.dcnt[semname] = self.dcnt.get(semname, 0) + 16
        if key is not None:
            self.tok[key] = ("d", semname, self.dcnt[semname])

    def alias(self, key, srckey):
        self.tok[key] = self.tok[srckey]

    def barrier(self):
        for e in ENGS:
            if self.cnt[e] > 0 and e != self.cur:
                k = ("t", e)
                if self.wm.get(k, 0) < self.cnt[e]:
                    self.wm[k] = self.cnt[e]
                    self.eng.wait_ge(self.tick[e], self.cnt[e])
        for name, v in sorted(self.dcnt.items()):
            k = ("d", name)
            if v > 0 and self.wm.get(k, 0) < v:
                self.wm[k] = v
                self.eng.wait_ge(self.dsem[name], v)



def build_program(cfg: Cfg):
    nc = bass.Bass("TRN2", target_bir_lowering=False)
    S = cfg.S
    NCH = cfg.NCH
    FAKE = cfg.FAKE

    xTb = nc.dram_tensor("xTb", [D, S], F32, kind="ExternalInput").ap()
    xTo = nc.dram_tensor("xTo", [D, OWN], F32, kind="ExternalInput").ap()
    posin = nc.dram_tensor("posin", [1, OWN], F32, kind="ExternalInput").ap()
    wqkv = nc.dram_tensor("wqkv", [D, NH * DQK + LORA], F32, kind="ExternalInput").ap()
    wkup = nc.dram_tensor("wkup", [LORA, NKV * DQK], F32, kind="ExternalInput").ap()
    wvup = nc.dram_tensor("wvup", [LORA, NKV * DV], F32, kind="ExternalInput").ap()
    wo = nc.dram_tensor("wo", [NH * DV, D], F32, kind="ExternalInput").ap()
    w1t = nc.dram_tensor("w1t", [D, HID], F32, kind="ExternalInput").ap()
    w3t = nc.dram_tensor("w3t", [D, HID], F32, kind="ExternalInput").ap()
    w2t = nc.dram_tensor("w2t", [HID, D], F32, kind="ExternalInput").ap()
    yT = nc.dram_tensor("yT", [D, OWN], F32, kind="ExternalOutput").ap()

    def inl(arr, name):
        return nc.inline_tensor(arr, name=name).ap()

    i_ones = inl(np.ones((128, 1), BF), "c_ones")
    i_onesbf = inl(np.ones((1, 128), BF), "c_onesbf")
    i_onesrow = inl(np.ones((1, 128), np.float32), "c_onesrow")
    ind4np = np.zeros((128, 4 * 4), np.float32)
    for g in range(4):
        ind4np[:, 4 * g + g] = 1.0
    i_ind4 = inl(ind4np.astype(BF), "c_ind4")
    i_ind4f = inl(ind4np, "c_ind4f")
    indrownp = np.zeros((4, 4 * 128), np.float32)
    for g in range(4):
        indrownp[g, 128 * g:128 * (g + 1)] = 1.0
    i_indrow = inl(indrownp, "c_indrow")
    i16n = np.zeros((128, 16 * NH), np.float32)
    for h in range(NH):
        i16n[:, 16 * h + h] = 1.0
    i_i16n = inl(i16n.astype(BF), "c_i16n")
    i16r = np.zeros((128, 16 * (NH // 2)), np.float32)
    for p_ in range(NH // 2):
        i16r[0:64, 16 * p_ + 2 * p_] = 1.0
        i16r[64:128, 16 * p_ + 2 * p_ + 1] = 1.0
    i_i16r = inl(i16r.astype(BF), "c_i16r")
    i4r = np.zeros((128, 4 * (NKV // 2)), np.float32)
    for p_ in range(NKV // 2):
        i4r[0:64, 4 * p_ + 2 * p_] = 1.0
        i4r[64:128, 4 * p_ + 2 * p_ + 1] = 1.0
    i_i4r = inl(i4r.astype(BF), "c_i4r")
    i_exp4 = inl(np.repeat(np.eye(4, dtype=np.float32), 4, axis=1), "c_exp4")
    i_h128 = inl(H128NP.astype(BF), "c_h128")
    h64 = np.zeros((64, 64), np.float32)
    for i_ in range(64):
        for j_ in range(64):
            h64[i_, j_] = 1.0 if bin(i_ & j_).count("1") % 2 == 0 else -1.0
    i_h64 = inl(h64.astype(BF), "c_h64")
    msk = np.zeros((128, 4 * 512), np.float32)
    for k in range(4):
        msk[:, 512 * k:512 * (k + 1)] = (
            np.arange(512)[None, :] - np.arange(128)[:, None] >= 128 * k)
    i_masks = inl(msk.astype(BF), "c_masks")
    augv = np.ones((64, NCH * 128), np.float32)
    for s_ in range(cfg.NSLAB):
        for c_ in range(NCH):
            if c_ >= 4 * s_:
                augv[4 * s_:4 * s_ + 4, 128 * c_:128 * (c_ + 1)] = BIGAUG
    i_augv = inl(augv.astype(BF), "c_augv")

    kvz_scr = nc.dram_tensor("kvz_scr", [NCH, NKV, 128, 384], BF16, kind="Internal").ap()
    kvz_cg = kvz_scr.rearrange("c g p w -> (c g) p w")
    kn_scr = kvz_scr[:, :, :, 0:128]
    kr_scr = kvz_scr[:, :, 0:65, 128:256]
    v_scr = kvz_scr[:, :, :, 256:384]
    w2h_scr = nc.dram_tensor("w2h_scr", [HC, 128, 8, 256], BF16, kind="Internal").ap()
    waA_scr = nc.dram_tensor("waA_scr", [8, HC, 128, 256], BF16, kind="Internal").ap()
    aug_dram = nc.dram_tensor("aug_dram", [1, (cfg.L[3] - 4) * 128], BF16,
                              kind="Internal").ap()
    qn_scr = nc.dram_tensor("qn_scr", [4, NH, 128, 512], BF16, kind="Internal").ap()
    qr_scr = nc.dram_tensor("qr_scr", [4, NH, 65, 512], BF16, kind="Internal").ap()
    ot_scr = nc.dram_tensor("ot_scr", [NH, 128, OWN], BF16, kind="Internal").ap()
    x2_scr = nc.dram_tensor("x2_scr", [DC, 128, OWN], F32, kind="Internal").ap()
    hid_scr = nc.dram_tensor("hid_scr", [HC, 128, OWN], BF16, kind="Internal").ap()

    pid = nc.partition_id()
    s4_sv = []
    s16_sv = []
    for e in range(4):
        if cfg.SPC > 1:
            expr4 = 4 * (cfg.SPC * e) + 4 * ((pid % 4 + e) % cfg.SPC)
        else:
            expr4 = 4 * e + pid * 0
        s4_sv.append(nc.snap(expr4, min_val=0, max_val=4 * (cfg.NSLAB - 1)))
        s16_sv.append(nc.snap(4 * expr4, min_val=0, max_val=16 * (cfg.NSLAB - 1)))



    MANUAL_BYTES = 204800
    _rng = nc.bump_sbuf(MANUAL_BYTES, align_bytes=64)
    assert _rng is not None
    MBASE = _rng[0]
    sbat = nc.alloc_sbuf_tensor_at

    class Bump:
        def __init__(self, base):
            self.off = base

        def al(self, name, shape, dtype, align=64):
            self.off = (self.off + align - 1) // align * align
            t = sbat(name, shape, dtype, offset=MBASE + self.off)
            nbytes = int(np.prod(shape[1:])) * (2 if dtype == BF16 else 4)
            self.off += nbytes
            assert self.off <= MANUAL_BYTES, (name, self.off)
            return t

    perm = Bump(0)
    c_ones = perm.al("c_ones", [128, 1], BF16)
    c_onesbf = perm.al("c_onesbf", [1, 128], BF16)
    c_onesrow = perm.al("c_onesrow", [1, 128], F32)
    c_ind4 = perm.al("c_ind4", [128, 16], BF16)
    c_ind4f = perm.al("c_ind4f", [128, 16], F32)
    c_indrow = perm.al("c_indrow", [4, 512], F32)
    c_i16n = perm.al("c_i16n", [128, 256], BF16)
    c_i16r = perm.al("c_i16r", [128, 128], BF16)
    c_i4r = perm.al("c_i4r", [128, 8], BF16)
    c_exp4 = perm.al("c_exp4", [4, 16], F32)
    c_h128 = perm.al("c_h128", [128, 128], BF16)
    c_h64 = perm.al("c_h64", [64, 64], BF16)
    c_masks = perm.al("c_masks", [128, 2048], BF16)
    wsum = perm.al("wsum", [128, 4], F32)
    kmax = perm.al("kmax", [4, 1], F32)
    nkmax = perm.al("nkmax", [16, 1], F32)
    rowsA = perm.al("rowsA", [1, 512], F32)
    rowsB = perm.al("rowsB", [1, 512], F32)
    rowsC = perm.al("rowsC", [1, 512], F32)
    zt128 = perm.al("zt128", [128, 128], BF16)
    P0 = ((perm.off + 63) // 64 * 64)

    w_ = Bump(P0)
    wpr = [w_.al(f"wpr{i}", [128, HC * 256], BF16) for i in range(2)]
    wev = [w_.al(f"wev{i}", [128, 512], BF16) for i in range(2)]
    wBi = [w_.al(f"wBi{i}", [64, 16 * 256], BF16) for i in range(2)]
    wBo = [w_.al(f"wBo{i}", [128, 4 * 512], BF16) for i in range(2)]

    m_ = Bump(P0)
    mcos = m_.al("mcos", [32, S], F32)
    msin = m_.al("msin", [32, S], F32)
    ocos = m_.al("ocos", [32, OWN], F32)
    osin = m_.al("osin", [32, OWN], F32)
    P1 = ((m_.off + 63) // 64 * 64)

    t_ = Bump(P1)
    mt_iota = t_.al("mt_iota", [1, S], F32)
    mt_invf = t_.al("mt_invf", [1, 32], F32)
    tth = t_.al("tth", [32, S], F32)
    ttk = t_.al("ttk", [32, S], F32)

    a_ = Bump(P1)
    xs_sb = [a_.al(f"xs{i}", [128, DC * 512], BF16) for i in range(2)]
    csb = a_.al("csb", [128, CC * 512], BF16)
    cosD = a_.al("cosD", [128, 512], F32)
    sinD = a_.al("sinD", [128, 512], F32)
    rraw2 = [a_.al(f"rraw{i}", [128, 512], F32) for i in range(2)]
    rswp2 = [a_.al(f"rswp{i}", [128, 512], F32) for i in range(2)]
    rt1_s = a_.al("rt1", [128, 512], F32)
    rt2_s = a_.al("rt2", [128, 512], F32)
    rt1_2 = [rt1_s, rt1_s]
    rt2_2 = [rt2_s, rt2_s]
    rt1 = rt1_s
    rt2 = rt2_s
    rout2 = [a_.al(f"rout{i}", [128, 512], BF16) for i in range(2)]
    rraw = rraw2[0]
    sqs = a_.al("sqs", [128, 512], BF16)
    kev = a_.al("kev", [128, 512], BF16)
    sS = a_.al("sS", [128, 512], F32)
    sM = a_.al("sM", [128, 512], F32)
    xnf = a_.al("xnf", [128, 512], F32)
    PA = ((a_.off + 63) // 64 * 64)
    aw = Bump(PA)
    wc_sb = aw.al("wc_sb", [128, DC * 512], BF16)
    wk_sb = aw.al("wk_sb", [128, CC * NKV * DQK], BF16)
    wv_sb = aw.al("wv_sb", [128, CC * NKV * DV], BF16)
    sqx = aw.al("sqx", [128, 8 * 512], BF16)
    aq = Bump(PA)
    wqs = [aq.al(f"wqs{i}", [128, DC * 128], BF16) for i in range(2)]
    xno = aq.al("xno", [128, DC * 512], BF16)
    qaug = aq.al("qaug", [65, NH * 512], BF16)
    qn2e = aq.al("qn2e", [16, 512], F32)
    qev = aq.al("qev", [128, 512], BF16)

    b_ = Bump(P0)
    TMAX = cfg.L[3] - 4
    b_tail = [b_.al(f"b_tail{i}", [128, TMAX * 384], BF16) for i in range(2)]
    b_diag = [b_.al(f"b_diag{i}", [128, 16 * 384], BF16) for i in range(2)]

    qn_sb = b_.al("qn_sb", [128, NH * 512], BF16)
    qr_sb = b_.al("qr_sb", [65, NH * 512], BF16)
    ex_sb = [b_.al(f"ex_sb{i}", [128, 4 * 512], BF16) for i in range(2)]
    exa_sb = b_.al("exa_sb", [128, 4 * 512], F32)
    rcp_sb = b_.al("rcp_sb", [4, 512], F32)
    bce_sb = b_.al("bce_sb", [128, 512], F32)
    otf_sb = [b_.al(f"otf_sb{i}", [128, 512], BF16) for i in range(2)]

    c1 = Bump(P0)
    wo_sb = c1.al("wo_sb", [128, NH * 128], BF16)
    c1 = Bump(P0)
    wo_sb = sbat("wo_sb2", [128, NH * D], BF16, offset=MBASE + P0)
    otx_sb = sbat("otx_sb", [128, NH * OWN], BF16, offset=MBASE + P0 + NH * D * 2)
    _c1off = P0 + NH * D * 2 + NH * OWN * 2
    xoc = [sbat(f"xoc{i}", [128, OWN], F32, offset=MBASE + _c1off + i * OWN * 4) for i in range(2)]
    x2e = sbat("x2e", [128, OWN], F32, offset=MBASE + _c1off + 2 * OWN * 4)
    assert _c1off + 3 * OWN * 4 <= MANUAL_BYTES

    c2 = Bump(P0)
    xn2 = c2.al("xn2", [128, DC * OWN], BF16)
    S2b = c2.al("S2b", [128, OWN], F32)
    M2b = c2.al("M2b", [128, OWN], F32)
    rows2 = c2.al("rows2", [1, OWN], F32)
    rows2b = c2.al("rows2b", [1, OWN], F32)
    rows2c = c2.al("rows2c", [1, OWN], F32)
    x2b = [c2.al(f"x2b{i}", [128, OWN], BF16) for i in range(2)]
    sq2 = [c2.al(f"sq2{i}", [128, OWN], BF16) for i in range(2)]
    x2f = [c2.al(f"x2f{i}", [128, OWN], F32) for i in range(2)]
    w1s = [c2.al(f"w1s{i}", [128, DC * 128], BF16) for i in range(2)]
    w3s = [c2.al(f"w3s{i}", [128, DC * 128], BF16) for i in range(2)]
    silu = c2.al("silu", [128, OWN], F32)
    hidsb = c2.al("hidsb", [128, OWN], BF16)

    c3 = Bump(P0)
    hidq = c3.al("hidq", [128, HC * 512], BF16)
    w2hs = [c3.al(f"w2hs{i}", [128, HC * 256], BF16) for i in range(2)]
    x2c = [c3.al(f"x2c{i}", [128, 512], F32) for i in range(2)]
    y_sb = [c3.al(f"y_sb{i}", [128, 512], F32) for i in range(2)]

    pz = nc.alloc_psum_tensor
    ps = [pz(f"ps{i}", [128, 512], F32) for i in range(8)]
    psA, psB, psC, psD, psE, psF, psG, psH = ps

    p = P(nc)

    def walk(pp: P):
        cl = [(c_ones, i_ones), (c_onesbf, i_onesbf), (c_onesrow, i_onesrow),
              (c_ind4, i_ind4), (c_ind4f, i_ind4f), (c_indrow, i_indrow), (c_i16n, i_i16n),
              (c_i16r, i_i16r), (c_i4r, i_i4r), (c_exp4, i_exp4),
              (c_h128, i_h128), (c_h64, i_h64), (c_masks, i_masks)]
        for ci_, (dst, src) in enumerate(cl):
            pp.dma("sync", "lc", key="k_consts" if ci_ == len(cl) - 1 else None,
                   fn=lambda e, dst=dst, src=src: e.dma_start(out=dst[:], in_=src))
        pp.pool(lambda e: e.memset(zt128[:], 0.0), key="k_zt")
        for g in range(NKV):
            pp.dma("pool", "fk", key=f"k_pad_{g}", after=("k_zt",),
                   fn=lambda e, g=g: e.dma_start(
                       out=kvz_scr[:, g, 65:128, 128:256].rearrange("c r w -> r c w"),
                       in_=zt128[0:63, :].rearrange("r (c w) -> r c w", c=1)
                       .broadcast_to([63, NCH, 128])))
            pp.dma("pool", "fk", key=f"k_augr_{g}", after=("k_consts",),
                   fn=lambda e, g=g: e.dma_start(
                       out=kvz_scr[0:NCH, g, 64:65, 128:256].rearrange("c one w -> one c w"),
                       in_=c_onesbf[0:1, :].rearrange("one (c w) -> one c w", c=1)
                       .broadcast_to([1, NCH, 128])))

        for sl in range(10):
            if sl < 8:
                wp = sl % 2
                pp.dma("pool", f"w2in{wp}", key=f"k_w2in_{sl}",
                       after=(f"k_wmm_{sl - 2}_{31}",) if sl >= 2 else (),
                       fn=lambda e, sl=sl, wp=wp: e.dma_start(
                           out=wpr[wp][:].rearrange("p (c w) -> p c w", c=HC),
                           in_=w2t[:, 256 * sl:256 * (sl + 1)]
                           .rearrange("(c p) w -> p c w", p=128)))
                for half in range(HC // 2):
                    c0 = 2 * half
                    ep = half % 2
                    dep = [f"k_w2in_{sl}", "k_consts"]
                    if half >= 2 or sl >= 1:
                        dep.append(f"k_wev_{sl}_{half - 2}" if half >= 2 else
                                   f"k_wev_{sl - 1}_{HC // 2 - 2 + half}")
                    pw = psC if half % 2 == 0 else psD
                    pp.mm(lambda e, c0=c0, pw=pw, wp=wp: e.matmul(
                        pw[:], c_h128[:], wpr[wp][:, 256 * c0:256 * (c0 + 2)],
                        start=True, stop=True), key=f"k_wmm_{sl}_{half}", after=tuple(dep))
                    wodep = ()
                    if half >= 2:
                        wodep = (f"k_w2o_{sl}_{half - 2}",)
                    elif sl >= 1:
                        wodep = (f"k_w2o_{sl - 1}_{HC // 2 - 2 + half}",)
                    pp.act(lambda e, pw=pw, ep=ep: e.activation(
                        wev[ep][:], pw[:], AF.Copy, scale=FWHT_SCALE),
                        key=f"k_wev_{sl}_{half}", after=(f"k_wmm_{sl}_{half}",) + wodep)
                    pp.dma("sync", f"w2o{ep}", key=f"k_w2o_{sl}_{half}",
                           after=(f"k_wev_{sl}_{half}",),
                           fn=lambda e, sl=sl, c0=c0, ep=ep: e.dma_start(
                               out=waA_scr[sl, c0:c0 + 2].rearrange("c p w -> p c w"),
                               in_=wev[ep][:].rearrange("p (c w) -> p c w", c=2)))
            if sl >= 2:
                sb = sl - 2
                psB4 = [psE, psF, psG, psH]
                for pb in range(8):
                    bp = pb % 2
                    bdep = (f"k_w2o_{sb}_{HC // 2 - 1}", f"k_w2o_{sb}_{HC // 2 - 2}")
                    if pb >= 2:
                        bdep = bdep + (f"k_wbm_{sb}_{pb - 2}_{7}",)
                    elif sb >= 1:
                        bdep = bdep + (f"k_wbm_{sb - 1}_{6 + pb}_{7}",)
                    pp.dma("sync", f"wbi{bp}", key=f"k_wbi_{sb}_{pb}", after=bdep,
                           fn=lambda e, sb=sb, pb=pb, bp=bp: e.dma_start(
                               out=wBi[bp][:],
                               in_=waA_scr[sb, :, 16 * pb:16 * (pb + 1), :]
                               .rearrange("c p w -> c (p w)")))
                    for f4 in range(4):
                        pq = psB4[f4]
                        mdep = ("k_consts", f"k_wbi_{sb}_{pb}")
                        if pb >= 1:
                            mdep = mdep + (f"k_wbe_{sb}_{pb - 1}_{f4}",)
                        elif sb >= 1:
                            mdep = mdep + (f"k_wbe_{sb - 1}_{7}_{f4}",)
                        pp.mm(lambda e, f4=f4, pq=pq, bp=bp: e.matmul(
                            pq[0:64, :], c_h64[:],
                            wBi[bp][:, 512 * f4:512 * (f4 + 1)],
                            start=True, stop=True),
                            key=None, after=mdep)
                        pp.mm(lambda e, f4=f4, pq=pq, bp=bp: e.matmul(
                            pq[64:128, :], c_h64[:],
                            wBi[bp][:, 512 * (f4 + 4):512 * (f4 + 5)],
                            start=True, stop=True),
                            key=f"k_wbm_{sb}_{pb}_{f4}")
                        pp.act(lambda e, f4=f4, pq=pq, bp=bp: e.activation(
                            wBo[bp][:, 512 * f4:512 * (f4 + 1)], pq[:], AF.Copy),
                            key=f"k_wbe_{sb}_{pb}_{f4}",
                            after=(f"k_wbm_{sb}_{pb}_{f4}",) +
                            ((f"k_wbo_{sb}_{pb - 2}",) if pb >= 2 else
                             ((f"k_wbo_{sb - 1}_{6 + pb}",) if sb >= 1 else ())))
                    pp.alias(f"k_wbm_{sb}_{pb}_{7}", f"k_wbm_{sb}_{pb}_{3}")
                    pp.dma("sync", f"wbo{bp}", key=f"k_wbo_{sb}_{pb}_a",
                           after=tuple(f"k_wbe_{sb}_{pb}_{f4}" for f4 in range(4)),
                           fn=lambda e, sb=sb, pb=pb, bp=bp: e.dma_start(
                               out=w2h_scr[:, 16 * pb:16 * pb + 8, sb, :],
                               in_=wBo[bp][0:64, :].rearrange("c (p w) -> c p w", p=8)))
                    pp.dma("sync", f"wbo{bp}", key=f"k_wbo_{sb}_{pb}",
                           fn=lambda e, sb=sb, pb=pb, bp=bp: e.dma_start(
                               out=w2h_scr[:, 16 * pb + 8:16 * (pb + 1), sb, :],
                               in_=wBo[bp][64:128, :].rearrange("c (p w) -> c p w", p=8)))

        pp.barrier()

        pp.pool(lambda e: e.iota(mt_iota[:], pattern=[[1, S]], base=0, channel_multiplier=0,
                                 allow_small_or_imprecise_dtypes=True), key="k_iota")
        pp.pool(lambda e: e.iota(mt_invf[:], pattern=[[1, 32]], base=0, channel_multiplier=0,
                                 allow_small_or_imprecise_dtypes=True), key="k_iinvf")
        pp.act(lambda e: e.activation(mt_invf[:], mt_invf[:], AF.Exp, scale=INVF_SCALE),
               key="k_invf", after=("k_iinvf",))

        def gen_trig(tag, n, cosT, sinT, rowdep, extra=()):
            nb = n // 512
            for b2 in range(nb):
                pp.mm(lambda e, b2=b2: e.matmul(
                    psA[0:32, :], mt_invf[:], mt_iota[0:1, 512 * b2:512 * (b2 + 1)],
                    start=True, stop=True), key=f"k_th_{tag}_{b2}",
                    after=("k_invf", rowdep) + tuple(extra) +
                    ((f"k_the_{tag}_{b2 - 1}",) if b2 >= 1 else ()))
                pp.act(lambda e, b2=b2: e.activation(
                    tth[0:32, 512 * b2:512 * (b2 + 1)], psA[0:32, :], AF.Copy),
                    key=f"k_the_{tag}_{b2}", after=(f"k_th_{tag}_{b2}",))
            kth = f"k_the_{tag}_{nb - 1}"
            pp.dve(lambda e: e.tensor_scalar(ttk[0:32, 0:n], tth[0:32, 0:n],
                                             float(1.0 / TWO_PI), None, ALU.mult),
                   key=f"k_k1_{tag}", after=(kth,))
            pp.dve(lambda e: e.tensor_scalar(ttk[0:32, 0:n], ttk[0:32, 0:n],
                                             MAGIC, MAGIC, ALU.add, ALU.subtract),
                   key=f"k_k2_{tag}", after=(f"k_k1_{tag}",))
            prev = f"k_k2_{tag}"
            for ci, cwc in enumerate((CW1, CW2, CW3)):
                pp.dve(lambda e, cwc=cwc: e.scalar_tensor_tensor(
                    tth[0:32, 0:n], ttk[0:32, 0:n], -cwc, tth[0:32, 0:n],
                    ALU.mult, ALU.add), key=f"k_cw{ci}_{tag}", after=(prev,))
                prev = f"k_cw{ci}_{tag}"
            pp.dve(lambda e: e.tensor_scalar(tth[0:32, 0:n], tth[0:32, 0:n],
                                             float(np.pi), float(-np.pi), ALU.min, ALU.max),
                   key=f"k_cl1_{tag}", after=(prev,))
            prev = f"k_cl1_{tag}"
            pp.act(lambda e: e.activation(sinT[:, 0:n], tth[0:32, 0:n], AF.Sin),
                   key=f"k_sin_{tag}", after=(prev,))
            pp.dve(lambda e: e.tensor_scalar(ttk[0:32, 0:n], tth[0:32, 0:n],
                                             float(np.pi / 2), None, ALU.add),
                   key=f"k_cs1_{tag}", after=(prev,))
            pp.dve(lambda e: e.tensor_scalar(tth[0:32, 0:n], ttk[0:32, 0:n],
                                             float(np.pi), None, ALU.is_gt),
                   key=f"k_cs2_{tag}", after=(f"k_cs1_{tag}", f"k_sin_{tag}"))
            pp.dve(lambda e: e.scalar_tensor_tensor(
                ttk[0:32, 0:n], tth[0:32, 0:n], -TWO_PI, ttk[0:32, 0:n],
                ALU.mult, ALU.add), key=f"k_cs3_{tag}", after=(f"k_cs2_{tag}",))
            pp.dve(lambda e: e.tensor_scalar(ttk[0:32, 0:n], ttk[0:32, 0:n],
                                             float(np.pi), float(-np.pi), ALU.min, ALU.max),
                   key=f"k_cl2_{tag}", after=(f"k_cs3_{tag}",))
            pp.act(lambda e: e.activation(cosT[:, 0:n], ttk[0:32, 0:n], AF.Sin),
                   key=f"k_cos_{tag}", after=(f"k_cl2_{tag}",))

        gen_trig("m", S, mcos, msin, "k_iota")
        pp.dma("sync", "pos", key="k_pos", after=(f"k_th_m_{S // 512 - 1}",),
               fn=lambda e: e.dma_start(out=mt_iota[0:1, 0:OWN], in_=posin))
        gen_trig("o", OWN, ocos, osin, "k_pos", extra=("k_cos_m",))
        pp.barrier()

        pp.dma("pool", "wc", key="k_wc", fn=lambda e: e.dma_start(
            out=wc_sb[:].rearrange("p (c f) -> p c f", c=DC),
            in_=wqkv[:, NH * DQK:].rearrange("(c p) f -> p c f", p=128)))
        pp.dma("pool", "wk", key="k_wk", fn=lambda e: e.dma_start(
            out=wk_sb[:].rearrange("p (c f) -> p c f", c=CC),
            in_=wkup.rearrange("(c p) f -> p c f", p=128)))
        pp.dma("pool", "wv", key="k_wv", fn=lambda e: e.dma_start(
            out=wv_sb[:].rearrange("p (c f) -> p c f", c=CC),
            in_=wvup.rearrange("(c p) f -> p c f", p=128)))
        for fc in range(CC):
            for dcx in range(DC):
                pp.mm(lambda e, fc=fc, dcx=dcx: e.matmul(
                    psB[:, 0:1],
                    wc_sb[:, 512 * dcx + 128 * fc:512 * dcx + 128 * fc + 128],
                    c_ones[:], start=(dcx == 0), stop=(dcx == DC - 1)),
                    key=f"k_wsm_{fc}" if dcx == DC - 1 else None,
                    after=("k_wc", "k_consts",
                           f"k_wsum_{fc - 1}" if fc >= 1 else None) if dcx == 0 else ())
            pp.act(lambda e, fc=fc: e.activation(wsum[:, fc:fc + 1], psB[:, 0:1], AF.Copy),
                   key=f"k_wsum_{fc}", after=(f"k_wsm_{fc}",))
        pp.pool(lambda e: e.memset(kmax[:], 0.0), key="k_km0")

        def load_xs(sp):
            pp.dma("pool", f"xs{sp % 2}", key=f"k_xs_{sp}",
                   after=(f"k_cr_{sp - 2}_{CC - 1}",) if sp >= 2 else (),
                   fn=lambda e, sp=sp: e.dma_start(
                       out=xs_sb[sp % 2][:].rearrange("p (c t) -> p c t", c=DC),
                       in_=xTb[:, 512 * sp:512 * (sp + 1)]
                       .rearrange("(c p) t -> p c t", p=128)))

        def emit_stats(sp):
            xs2 = xs_sb[sp % 2]
            for dcx in range(DC):
                dep0 = (f"k_xs_{sp}", "k_consts")
                if sp >= 1:
                    dep0 = dep0 + (f"k_rowA_{sp - 1}",)
                pp.mm(lambda e, dcx=dcx, xs2=xs2: e.matmul(
                    psG[0:1, :], c_ones[:], xs2[:, 512 * dcx:512 * (dcx + 1)],
                    start=(dcx == 0), stop=(dcx == DC - 1)),
                    key=f"k_sx_{sp}" if dcx == DC - 1 else None,
                    after=dep0 if dcx == 0 else ())
            for dcx in range(DC):
                sl8 = dcx % 8
                sqdep = (f"k_xs_{sp}",)
                if dcx >= 8:
                    sqdep = sqdep + (f"k_sqm_{sp}_{dcx - 8}",)
                elif dcx == 0 and sp >= 1:
                    sqdep = sqdep + (f"k_sqm_{sp - 1}_{DC - 1}",)
                pp.dve(lambda e, sl8=sl8, dcx=dcx, xs2=xs2: e.tensor_mul(
                    sqx[:, 512 * sl8:512 * (sl8 + 1)],
                    xs2[:, 512 * dcx:512 * (dcx + 1)],
                    xs2[:, 512 * dcx:512 * (dcx + 1)]),
                    key=f"k_sq_{sp}_{dcx}", after=sqdep)
                mdep = (f"k_sq_{sp}_{dcx}",)
                if dcx == 0 and sp >= 1:
                    mdep = mdep + (f"k_rowB_{sp - 1}",)
                pp.mm(lambda e, sl8=sl8, dcx=dcx: e.matmul(
                    psH[0:1, :], c_ones[:], sqx[:, 512 * sl8:512 * (sl8 + 1)],
                    start=(dcx == 0), stop=(dcx == DC - 1)),
                    key=f"k_sqm_{sp}_{dcx}", after=mdep)

        load_xs(0)
        load_xs(1)
        emit_stats(0)

        for s_ in range(cfg.NSLAB):
            par = s_ % 2
            xs = xs_sb[par]
            pp.act(lambda e: e.activation(rowsA[:], psG[0:1, :], AF.Copy, scale=1.0 / D),
                   key=f"k_rowA_{s_}", after=(f"k_sx_{s_}",) +
                   ((f"k_ms_{s_ - 1}",) if s_ >= 1 else ()))
            pp.act(lambda e: e.activation(rowsB[:], psH[0:1, :], AF.Copy, scale=1.0 / D),
                   key=f"k_rowB_{s_}", after=(f"k_sqm_{s_}_{DC - 1}",) +
                   ((f"k_ms_{s_ - 1}",) if s_ >= 1 else ()))
            if s_ + 1 < cfg.NSLAB:
                emit_stats(s_ + 1)
            pp.act(lambda e: e.activation(rowsC[:], rowsA[:], AF.Square),
                   key=f"k_rowC_{s_}", after=(f"k_rowA_{s_}",))
            pp.dve(lambda e: e.tensor_sub(rowsB[:], rowsB[:], rowsC[:]),
                   key=f"k_var_{s_}", after=(f"k_rowB_{s_}", f"k_rowC_{s_}"))
            pp.dve(lambda e: e.tensor_scalar(rowsB[:], rowsB[:], EPS, None, ALU.add),
                   key=f"k_vare_{s_}", after=(f"k_var_{s_}",))
            pp.act(lambda e: e.activation(rowsB[:], rowsB[:], AF.Ln),
                   key=f"k_lnv_{s_}", after=(f"k_vare_{s_}",))
            pp.act(lambda e: e.activation(rowsB[:], rowsB[:], AF.Exp, scale=-0.5),
                   key=f"k_istd_{s_}", after=(f"k_lnv_{s_}",))
            pp.dve(lambda e: e.tensor_mul(rowsC[:], rowsA[:], rowsB[:]),
                   key=f"k_ms_{s_}", after=(f"k_istd_{s_}", f"k_rowC_{s_}"))
            depE = (f"k_istd_{s_}", "k_consts")
            if s_ >= 1:
                depE = depE + (f"k_vev_{s_ - 1}_2",)
            pp.mm(lambda e: e.matmul(psE[:], c_onesrow[:], rowsB[:], start=True, stop=True),
                  key=f"k_bs_{s_}", after=depE)
            pp.act(lambda e: e.activation(sS[:], psE[:], AF.Copy),
                   key=f"k_bse_{s_}", after=(f"k_bs_{s_}",))
            pp.mm(lambda e: e.matmul(psE[:], c_onesrow[:], rowsC[:], start=True, stop=True),
                  key=f"k_bm_{s_}", after=(f"k_ms_{s_}", f"k_bse_{s_}"))
            pp.act(lambda e: e.activation(sM[:], psE[:], AF.Copy),
                   key=f"k_bme_{s_}", after=(f"k_bm_{s_}",))
            for fc in range(CC):
                pcs = psA if fc % 2 == 0 else psB
                for dcx in range(DC):
                    dep0 = [f"k_xs_{s_}", "k_wc"]
                    if fc <= 1 and s_ == 0:
                        dep0.append(f"k_wsum_{CC - 1}")
                    if s_ >= 1 or fc >= 2:
                        dep0.append(f"k_cf1_{s_}_{fc - 2}" if fc >= 2 else f"k_cf1_{s_ - 1}_{fc + 2}")
                    pp.mm(lambda e, fc=fc, dcx=dcx, xs=xs, pcs=pcs: e.matmul(
                        pcs[:], wc_sb[:, 512 * dcx + 128 * fc:512 * dcx + 128 * fc + 128],
                        xs[:, 512 * dcx:512 * (dcx + 1)],
                        start=(dcx == 0), stop=(dcx == DC - 1)),
                        key=f"k_cr_{s_}_{fc}" if dcx == DC - 1 else None,
                        after=tuple(dep0) if dcx == 0 else ())
                pp.dve(lambda e, pcs=pcs: e.tensor_mul(xnf[:], pcs[:], sS[:]),
                       key=f"k_cf1_{s_}_{fc}", after=(f"k_cr_{s_}_{fc}", f"k_bse_{s_}"))
                pp.dve(lambda e, fc=fc: e.tensor_scalar(
                    rt1[:], sM[:], wsum[:, fc:fc + 1], None, ALU.mult),
                    key=f"k_cf2_{s_}_{fc}",
                    after=(f"k_bme_{s_}", f"k_wsum_{fc}") +
                    ((f"k_cfix_{s_}_{fc - 1}",) if fc >= 1 else
                     ((f"k_cfix_{s_ - 1}_{CC - 1}",) if s_ >= 1 else ())))
                pp.dve(lambda e, fc=fc: e.tensor_sub(
                    csb[:, 512 * fc:512 * (fc + 1)], xnf[:], rt1[:]),
                    key=f"k_cfix_{s_}_{fc}",
                    after=(f"k_cf1_{s_}_{fc}", f"k_cf2_{s_}_{fc}"))
            cdone = f"k_cfix_{s_}_{CC - 1}"
            if s_ + 2 < cfg.NSLAB:
                load_xs(s_ + 2)
            for g in range(NKV):
                for ccx in range(CC):
                    dep0 = (cdone, "k_wk") if ccx == 0 else ()
                    if ccx == 0 and g >= 1:
                        dep0 = dep0 + (f"k_knsq_{s_}_{g - 1}",)
                    if ccx == 0 and g == 0 and s_ >= 1:
                        dep0 = dep0 + (f"k_knsq_{s_ - 1}_{NKV - 1}",)
                    pp.mm(lambda e, g=g, ccx=ccx: e.matmul(
                        psC[:],
                        wk_sb[:, (NKV * DQK) * ccx + 128 * g:(NKV * DQK) * ccx + 128 * g + 128],
                        csb[:, 512 * ccx:512 * (ccx + 1)],
                        start=(ccx == 0), stop=(ccx == CC - 1)),
                        key=f"k_kn_{s_}_{g}" if ccx == CC - 1 else None,
                        after=dep0)
                pp.act(lambda e: e.activation(kev[:], psC[:], AF.Copy),
                       key=f"k_knev_{s_}_{g}", after=(f"k_kn_{s_}_{g}",) +
                       ((f"k_knw_{s_}_{g - 1}_3",) if g >= 1 else
                        ((f"k_vw_{s_ - 1}_3_{NKV - 1}",) if s_ >= 1 else ())))
                pp.act(lambda e: e.activation(sqs[:], psC[:], AF.Square),
                       key=f"k_knsq_{s_}_{g}", after=(f"k_knev_{s_}_{g}",))
                dep_kn2 = (f"k_knsq_{s_}_{g}",)
                if g == 0:
                    if s_ >= 1:
                        dep_kn2 = dep_kn2 + (f"k_vev_{s_ - 1}_{3}", f"k_kmax_{s_ - 1}")
                pp.mm(lambda e, g=g: e.matmul(
                    psF[0:4, :], c_ind4[:, 4 * g:4 * g + 4], sqs[:],
                    start=(g == 0), stop=False), key=f"k_knm_{s_}_{g}", after=dep_kn2)
                pp.dma("sync", "knw", key=f"k_knw_{s_}_{g}_3",
                       after=(f"k_knev_{s_}_{g}",),
                       fn=lambda e, s_=s_, g=g: e.dma_start(
                           out=kn_scr[4 * s_:4 * s_ + 4, g].rearrange("c p w -> p c w"),
                           in_=kev[:].rearrange("p (c w) -> p c w", c=4)))
            for cx in range(4):
                pp.dma("sync", "trg", key=f"k_cosD_{s_}_{cx}",
                       after=("k_cos_m",) + ((f"k_rope_kr_{s_ - 1}_1",) if s_ >= 1 and cx == 0 else ()),
                       fn=lambda e, s_=s_, cx=cx: e.dma_start(
                           out=cosD[32 * cx:32 * (cx + 1), :],
                           in_=mcos[:, 512 * s_:512 * (s_ + 1)]))
                pp.dma("sync", "trg", key=f"k_sinD_{s_}_{cx}",
                       after=("k_sin_m",) + ((f"k_rope_kr_{s_ - 1}_1",) if s_ >= 1 and cx == 0 else ()),
                       fn=lambda e, s_=s_, cx=cx: e.dma_start(
                           out=sinD[32 * cx:32 * (cx + 1), :],
                           in_=msin[:, 512 * s_:512 * (s_ + 1)]))
            pp.act(lambda e: e.activation(sinD[0:32, :], sinD[0:32, :], AF.Copy, scale=-1.0),
                   key=f"k_sn1_{s_}", after=(f"k_sinD_{s_}_3", f"k_cosD_{s_}_3"))
            pp.act(lambda e: e.activation(sinD[64:96, :], sinD[64:96, :], AF.Copy, scale=-1.0),
                   key=f"k_sn2_{s_}", after=(f"k_sn1_{s_}",))
            for pr in range(NKV // 2):
                for ccx in range(CC):
                    dep0 = (cdone,) if ccx == 0 else ()
                    if ccx == 0 and pr >= 1:
                        dep0 = dep0 + (f"k_krsq_{s_}_{pr - 1}",)
                    if ccx == 0 and pr == 0 and s_ >= 1:
                        dep0 = dep0 + (f"k_krsq_{s_ - 1}_{NKV // 2 - 1}",)
                    pp.mm(lambda e, pr=pr, ccx=ccx: e.matmul(
                        psD[:],
                        wk_sb[:, (NKV * DQK) * ccx + NKV * DN + 128 * pr:
                              (NKV * DQK) * ccx + NKV * DN + 128 * pr + 128],
                        csb[:, 512 * ccx:512 * (ccx + 1)],
                        start=(ccx == 0), stop=(ccx == CC - 1)),
                        key=f"k_kr_{s_}_{pr}" if ccx == CC - 1 else None,
                        after=dep0)
                pp.act(lambda e, pr=pr: e.activation(rraw2[pr % 2][:], psD[:], AF.Copy),
                       key=f"k_krev_{s_}_{pr}", after=(f"k_kr_{s_}_{pr}",) +
                       ((f"k_rope_kr_{s_ - 1}_{pr}", f"k_swp_kr_{s_ - 1}_{pr}_3")
                        if s_ >= 1 else ()))
                pp.act(lambda e: e.activation(sqs[:], psD[:], AF.Square),
                       key=f"k_krsq_{s_}_{pr}", after=(f"k_krev_{s_}_{pr}",))
                pp.mm(lambda e, pr=pr: e.matmul(
                    psF[0:4, :], c_i4r[:, 4 * pr:4 * (pr + 1)], sqs[:],
                    start=False, stop=(pr == NKV // 2 - 1)),
                    key=f"k_krm_{s_}_{pr}", after=(f"k_krsq_{s_}_{pr}",))
                rwdep = ()
                if s_ >= 1:
                    rwdep = (f"k_krw_{s_ - 1}_{pr}_3", f"k_krw2_{s_ - 1}_{pr}_3")
                emit_rope(pp, f"kr_{s_}_{pr}",
                          deps=(f"k_krev_{s_}_{pr}", f"k_sn2_{s_}") + rwdep,
                          out_even=rout2[pr % 2][0:64, :],
                          out_odd=rout2[pr % 2][64:128, :],
                          use_rout=True, pb=pr % 2)
                pp.dma("sync", f"krw{pr}", key=f"k_krw_{s_}_{pr}_3",
                       after=(f"k_rope_kr_{s_}_{pr}",),
                       fn=lambda e, s_=s_, pr=pr: e.dma_start(
                           out=kr_scr[4 * s_:4 * s_ + 4, 2 * pr, 0:64, :]
                           .rearrange("c p w -> p c w"),
                           in_=rout2[pr % 2][0:64, :].rearrange("p (c w) -> p c w", c=4)))
                pp.dma("sync", f"krw{pr}", key=f"k_krw2_{s_}_{pr}_3",
                       after=(f"k_rope_kr_{s_}_{pr}",),
                       fn=lambda e, s_=s_, pr=pr: e.dma_start(
                           out=kr_scr[4 * s_:4 * s_ + 4, 2 * pr + 1, 0:64, :]
                           .rearrange("c p w -> p c w"),
                           in_=rout2[pr % 2][64:128, :].rearrange("p (c w) -> p c w", c=4)))
            pp.dve(lambda e: e.reduce_max(rt2[0:4, 0:1], psF[0:4, :], AXX),
                   key=f"k_kred_{s_}", after=(f"k_krm_{s_}_{NKV // 2 - 1}",))
            pp.dve(lambda e: e.tensor_max(kmax[:], kmax[:], rt2[0:4, 0:1]),
                   key=f"k_kmax_{s_}", after=(f"k_kred_{s_}", "k_km0"))
            for tc in range(4):
                pv = psE if tc % 2 == 0 else psF
                for ccx in range(CC):
                    dep0 = [cdone, "k_wv"]
                    if tc == 0:
                        dep0.append(f"k_bme_{s_}")
                    if tc == 1:
                        dep0.append(f"k_kred_{s_}")
                    if tc >= 2:
                        dep0.append(f"k_vev_{s_}_{tc - 2}")
                    elif s_ >= 1:
                        dep0.append(f"k_vev_{s_ - 1}_{tc + 2}")
                    pp.mm(lambda e, tc=tc, ccx=ccx, pv=pv: e.matmul(
                        pv[:], csb[:, 512 * ccx + 128 * tc:512 * ccx + 128 * tc + 128],
                        wv_sb[:, (NKV * DV) * ccx:(NKV * DV) * (ccx + 1)],
                        start=(ccx == 0), stop=(ccx == CC - 1)),
                        key=f"k_v_{s_}_{tc}" if ccx == CC - 1 else None,
                        after=tuple(dep0) if ccx == 0 else ())
                pp.act(lambda e, pv=pv: e.activation(kev[:], pv[:], AF.Copy),
                       key=f"k_vev_{s_}_{tc}", after=(f"k_v_{s_}_{tc}",) +
                       ((f"k_vw_{s_}_{tc - 1}_{NKV - 1}",) if tc >= 1 else
                        ((f"k_knw_{s_}_{NKV - 1}_3",) if True else ())))
                pp.dma("sync", "vw", key=f"k_vw_{s_}_{tc}_{NKV - 1}",
                       after=(f"k_vev_{s_}_{tc}",),
                       fn=lambda e, s_=s_, tc=tc: e.dma_start(
                           out=v_scr[4 * s_ + tc].rearrange("g p w -> p g w"),
                           in_=kev[:].rearrange("p (g w) -> p g w", g=4)))
            pp.alias(f"k_vq_{s_}", f"k_v_{s_}_3")
        pp.dve(lambda e: e.tensor_scalar(kmax[:], kmax[:], 1e-30, None, ALU.add),
               key="k_kma", after=(f"k_kmax_{cfg.NSLAB - 1}",))
        pp.act(lambda e: e.activation(kmax[:], kmax[:], AF.Ln),
               key="k_kml", after=("k_kma",))
        pp.act(lambda e: e.activation(kmax[:], kmax[:], AF.Exp, scale=0.5),
               key="k_kme", after=("k_kml",))
        pp.mm(lambda e: e.matmul(psG[0:16, 0:1], c_exp4[:], kmax[:], start=True, stop=True),
              key="k_kmx", after=("k_kme", f"k_kred_{cfg.NSLAB - 1}"))
        pp.act(lambda e: e.activation(nkmax[:], psG[0:16, 0:1], AF.Copy, scale=-1.0),
               key="k_nkm", after=("k_kmx",))
        pp.barrier()

        def load_xo(ep2):
            pp.dma("pool", f"xs{ep2 % 2}", key=f"k_xo_{ep2}",
                   after=(f"k_qxn_{ep2 - 2}_{DC - 1}",) if ep2 >= 2 else (),
                   fn=lambda e, ep2=ep2: e.dma_start(
                       out=xs_sb[ep2 % 2][:].rearrange("p (c t) -> p c t", c=DC),
                       in_=xTo[:, 512 * ep2:512 * (ep2 + 1)]
                       .rearrange("(c p) t -> p c t", p=128)))

        load_xo(0)
        load_xo(1)
        for e_ in range(4):
            par = e_ % 2
            xs = xs_sb[par]
            for dcx in range(DC):
                dep0 = (f"k_xo_{e_}",)
                if e_ >= 1:
                    dep0 = dep0 + (f"k_orowA_{e_ - 1}",)
                pp.mm(lambda e, dcx=dcx, xs=xs: e.matmul(
                    psG[0:1, :], c_ones[:], xs[:, 512 * dcx:512 * (dcx + 1)],
                    start=(dcx == 0), stop=(dcx == DC - 1)),
                    key=f"k_osx_{e_}" if dcx == DC - 1 else None,
                    after=dep0 if dcx == 0 else ())
                pp.act(lambda e, dcx=dcx, xs=xs: e.activation(
                    sqs[:], xs[:, 512 * dcx:512 * (dcx + 1)], AF.Square),
                    key=f"k_osq_{e_}_{dcx}",
                    after=(f"k_xo_{e_}",) if dcx == 0 else (f"k_osqm_{e_}_{dcx - 1}",))
                pp.mm(lambda e: e.matmul(psH[0:1, :], c_ones[:], sqs[:],
                                         start=(dcx == 0), stop=(dcx == DC - 1)),
                      key=f"k_osqm_{e_}_{dcx}",
                      after=(f"k_osq_{e_}_{dcx}",) +
                      ((f"k_orowB_{e_ - 1}",) if dcx == 0 and e_ >= 1 else ()))
            pp.act(lambda e: e.activation(rowsA[:], psG[0:1, :], AF.Copy, scale=1.0 / D),
                   key=f"k_orowA_{e_}", after=(f"k_osx_{e_}",) +
                   ((f"k_oms_{e_ - 1}",) if e_ >= 1 else ()))
            pp.act(lambda e: e.activation(rowsB[:], psH[0:1, :], AF.Copy, scale=1.0 / D),
                   key=f"k_orowB_{e_}", after=(f"k_osqm_{e_}_{DC - 1}",) +
                   ((f"k_oms_{e_ - 1}",) if e_ >= 1 else ()))
            pp.act(lambda e: e.activation(rowsC[:], rowsA[:], AF.Square),
                   key=f"k_orowC_{e_}", after=(f"k_orowA_{e_}",))
            pp.dve(lambda e: e.tensor_sub(rowsB[:], rowsB[:], rowsC[:]),
                   key=f"k_ovar_{e_}", after=(f"k_orowB_{e_}", f"k_orowC_{e_}"))
            pp.dve(lambda e: e.tensor_scalar(rowsB[:], rowsB[:], EPS, None, ALU.add),
                   key=f"k_ovare_{e_}", after=(f"k_ovar_{e_}",))
            pp.act(lambda e: e.activation(rowsB[:], rowsB[:], AF.Ln),
                   key=f"k_olnv_{e_}", after=(f"k_ovare_{e_}",))
            pp.act(lambda e: e.activation(rowsB[:], rowsB[:], AF.Exp, scale=-0.5),
                   key=f"k_oistd_{e_}", after=(f"k_olnv_{e_}",))
            pp.dve(lambda e: e.tensor_mul(rowsC[:], rowsA[:], rowsB[:]),
                   key=f"k_oms_{e_}", after=(f"k_oistd_{e_}", f"k_orowC_{e_}"))
            depE = (f"k_oistd_{e_}",)
            if e_ >= 1:
                depE = depE + (f"k_obme_{e_ - 1}",)
            pp.mm(lambda e: e.matmul(psE[:], c_onesrow[:], rowsB[:], start=True, stop=True),
                  key=f"k_obs_{e_}", after=depE)
            pp.act(lambda e: e.activation(sS[:], psE[:], AF.Copy),
                   key=f"k_obse_{e_}", after=(f"k_obs_{e_}",))
            pp.mm(lambda e: e.matmul(psE[:], c_onesrow[:], rowsC[:], start=True, stop=True),
                  key=f"k_obm_{e_}", after=(f"k_oms_{e_}", f"k_obse_{e_}"))
            pp.act(lambda e: e.activation(sM[:], psE[:], AF.Copy),
                   key=f"k_obme_{e_}", after=(f"k_obm_{e_}",))
            for dcx in range(DC):
                pp.dve(lambda e, dcx=dcx, xs=xs: e.tensor_mul(
                    xnf[:], xs[:, 512 * dcx:512 * (dcx + 1)], sS[:]),
                    key=f"k_qxm_{e_}_{dcx}",
                    after=(f"k_obse_{e_}",) if dcx == 0 else (f"k_qxn_{e_}_{dcx - 1}",))
                pp.dve(lambda e, dcx=dcx: e.tensor_sub(
                    xno[:, 512 * dcx:512 * (dcx + 1)], xnf[:], sM[:]),
                    key=f"k_qxn_{e_}_{dcx}",
                    after=(f"k_qxm_{e_}_{dcx}", f"k_obme_{e_}") +
                    ((f"k_qrmm_{e_ - 1}_{NH // 2 - 1}",) if dcx == 0 and e_ >= 1 else ()))
            xndone = f"k_qxn_{e_}_{DC - 1}"
            for cx in range(4):
                pp.dma("sync", "trg", key=f"k_ocosD_{e_}_{cx}",
                       after=("k_cos_o",) + ((f"k_rope_qr_{e_ - 1}_{NH // 2 - 1}",)
                                             if e_ >= 1 and cx == 0 else ()),
                       fn=lambda e, e_=e_, cx=cx: e.dma_start(
                           out=cosD[32 * cx:32 * (cx + 1), :],
                           in_=ocos[:, 512 * e_:512 * (e_ + 1)]))
                pp.dma("sync", "trg", key=f"k_osinD_{e_}_{cx}",
                       after=("k_sin_o",) + ((f"k_rope_qr_{e_ - 1}_{NH // 2 - 1}",)
                                             if e_ >= 1 and cx == 0 else ()),
                       fn=lambda e, e_=e_, cx=cx: e.dma_start(
                           out=sinD[32 * cx:32 * (cx + 1), :],
                           in_=osin[:, 512 * e_:512 * (e_ + 1)]))
            pp.act(lambda e: e.activation(sinD[0:32, :], sinD[0:32, :], AF.Copy, scale=-1.0),
                   key=f"k_osn1_{e_}", after=(f"k_osinD_{e_}_3", f"k_ocosD_{e_}_3"))
            pp.act(lambda e: e.activation(sinD[64:96, :], sinD[64:96, :], AF.Copy, scale=-1.0),
                   key=f"k_osn2_{e_}", after=(f"k_osn1_{e_}",))
            for h in range(NH):
                wpar = h % 2
                pq = psA if h % 2 == 0 else psB
                wdep = (f"k_qnmm_{e_}_{h - 2}",) if h >= 2 else \
                    ((f"k_qrmm_{e_ - 1}_{NH // 2 - 2 + h}",) if e_ >= 1 else ())
                pp.dma("pool", f"wq{wpar}", key=f"k_wq_{e_}_{h}", after=wdep,
                       fn=lambda e, h=h, wpar=wpar: e.dma_start(
                           out=wqs[wpar][:].rearrange("p (c f) -> p c f", c=DC),
                           in_=wqkv[:, 128 * h:128 * (h + 1)].rearrange("(c p) f -> p c f", p=128)))
                for dcx in range(DC):
                    dep0 = (f"k_wq_{e_}_{h}", f"k_qxn_{e_}_{dcx}")
                    if h >= 2 and dcx == 0:
                        dep0 = dep0 + (f"k_qsq_{e_}_{h - 2}",)
                    pp.mm(lambda e, dcx=dcx, wpar=wpar, pq=pq: e.matmul(
                        pq[:], wqs[wpar][:, 128 * dcx:128 * (dcx + 1)],
                        xno[:, 512 * dcx:512 * (dcx + 1)],
                        start=(dcx == 0), stop=(dcx == DC - 1)),
                        key=f"k_qnmm_{e_}_{h}" if dcx == DC - 1 else None,
                        after=dep0)
                pp.act(lambda e, pq=pq: e.activation(qev[:], pq[:], AF.Copy, scale=SCALE),
                       key=f"k_qev_{e_}_{h}", after=(f"k_qnmm_{e_}_{h}",) +
                       ((f"k_qnw_{e_}_{h - 1}",) if h >= 1 else
                        ((f"k_qnw_{e_ - 1}_{NH - 1}",) if e_ >= 1 else ())))
                pp.act(lambda e, pq=pq: e.activation(sqs[:], pq[:], AF.Square, scale=SCALE),
                       key=f"k_qsq_{e_}_{h}", after=(f"k_qev_{e_}_{h}",))
                dep_q16 = (f"k_qsq_{e_}_{h}",)
                if h == 0:
                    dep_q16 = dep_q16 + ("k_nkm",)
                    if e_ >= 1:
                        dep_q16 = dep_q16 + (f"k_mhat_{e_ - 1}",)
                pp.mm(lambda e, h=h: e.matmul(
                    psF[0:16, :], c_i16n[:, 16 * h:16 * (h + 1)], sqs[:],
                    start=(h == 0), stop=False), key=f"k_qn16_{e_}_{h}", after=dep_q16)
                pp.dma("pool", "qnw", key=f"k_qnw_{e_}_{h}", after=(f"k_qev_{e_}_{h}",),
                       fn=lambda e, e_=e_, h=h: e.dma_start(out=qn_scr[e_, h], in_=qev[:]))
            if e_ + 2 < 4:
                load_xo(e_ + 2)
            for pr in range(NH // 2):
                wpar = pr % 2
                pq = psA if pr % 2 == 0 else psB
                pp.dma("pool", f"wq{wpar}", key=f"k_wqr_{e_}_{pr}",
                       after=(f"k_qrmm_{e_}_{pr - 2}",) if pr >= 2 else
                       (f"k_qnmm_{e_}_{NH - 2 + pr}",),
                       fn=lambda e, pr=pr, wpar=wpar: e.dma_start(
                           out=wqs[wpar][:].rearrange("p (c f) -> p c f", c=DC),
                           in_=wqkv[:, NH * DN + 128 * pr:NH * DN + 128 * (pr + 1)]
                           .rearrange("(c p) f -> p c f", p=128)))
                for dcx in range(DC):
                    dep0 = (f"k_wqr_{e_}_{pr}",)
                    if pr >= 2:
                        dep0 = dep0 + (f"k_qrsq_{e_}_{pr - 2}",)
                    else:
                        dep0 = dep0 + (f"k_qsq_{e_}_{NH - 2 + pr}",)
                    pp.mm(lambda e, dcx=dcx, wpar=wpar, pq=pq: e.matmul(
                        pq[:], wqs[wpar][:, 128 * dcx:128 * (dcx + 1)],
                        xno[:, 512 * dcx:512 * (dcx + 1)],
                        start=(dcx == 0), stop=(dcx == DC - 1)),
                        key=f"k_qrmm_{e_}_{pr}" if dcx == DC - 1 else None,
                        after=dep0 if dcx == 0 else ())
                pp.act(lambda e, pq=pq, pr=pr: e.activation(
                    rraw2[pr % 2][:], pq[:], AF.Copy, scale=SCALE),
                    key=f"k_qrev_{e_}_{pr}", after=(f"k_qrmm_{e_}_{pr}",) +
                    ((f"k_rope_qr_{e_}_{pr - 2}", f"k_swp_qr_{e_}_{pr - 2}_3") if pr >= 2 else
                     ((f"k_rope_qr_{e_ - 1}_{NH // 2 - 2 + pr}",
                       f"k_swp_qr_{e_ - 1}_{NH // 2 - 2 + pr}_3") if e_ >= 1 else ())))
                pp.act(lambda e, pq=pq: e.activation(sqs[:], pq[:], AF.Square, scale=SCALE),
                       key=f"k_qrsq_{e_}_{pr}", after=(f"k_qrev_{e_}_{pr}",))
                pp.mm(lambda e, pr=pr: e.matmul(
                    psF[0:16, :], c_i16r[:, 16 * pr:16 * (pr + 1)], sqs[:],
                    start=False, stop=(pr == NH // 2 - 1)),
                    key=f"k_qr16_{e_}_{pr}", after=(f"k_qrsq_{e_}_{pr}",))
                emit_rope(pp, f"qr_{e_}_{pr}",
                          deps=(f"k_qrev_{e_}_{pr}", f"k_osn2_{e_}") +
                          ((f"k_qrw_{e_ - 1}",) if e_ >= 1 and pr <= 1 else ()),
                          out_even=qaug[0:64, 512 * (2 * pr):512 * (2 * pr) + 512],
                          out_odd=qaug[0:64, 512 * (2 * pr + 1):512 * (2 * pr + 1) + 512],
                          use_rout=False, pb=pr % 2)
            pp.act(lambda e: e.activation(qn2e[:], psF[0:16, :], AF.Copy),
                   key=f"k_qcp_{e_}", after=(f"k_qr16_{e_}_{NH // 2 - 1}",))
            pp.dve(lambda e: e.tensor_scalar(qn2e[:], qn2e[:], 1e-30, None, ALU.add),
                   key=f"k_qeps_{e_}", after=(f"k_qcp_{e_}",))
            pp.act(lambda e: e.activation(qn2e[:], qn2e[:], AF.Ln),
                   key=f"k_qln_{e_}", after=(f"k_qeps_{e_}",))
            pp.act(lambda e: e.activation(qn2e[:], qn2e[:], AF.Exp, scale=0.5),
                   key=f"k_qnorm_{e_}", after=(f"k_qln_{e_}",))
            pp.dve(lambda e: e.tensor_scalar(qn2e[:], qn2e[:], nkmax[:], None, ALU.mult),
                   key=f"k_mhat_{e_}", after=(f"k_qnorm_{e_}", "k_nkm"))
            for h in range(NH):
                pp.dma("pool", "mh", key=f"k_mh_{e_}_{h}",
                       after=(f"k_mhat_{e_}", f"k_rope_qr_{e_}_{NH // 2 - 1}") +
                       ((f"k_qrw_{e_ - 1}",) if e_ >= 1 and h == 0 else ()),
                       fn=lambda e, h=h: e.dma_start(
                           out=qaug[64:65, 512 * h:512 * (h + 1)],
                           in_=qn2e[h:h + 1, :]))
            pp.dma("pool", "qrw", key=f"k_qrw_{e_}",
                   after=tuple(f"k_mh_{e_}_{h}" for h in range(NH)),
                   fn=lambda e, e_=e_: e.dma_start(
                       out=qr_scr[e_].rearrange("h r t -> r h t"),
                       in_=qaug[:].rearrange("r (h t) -> r h t", h=NH)))
        pp.barrier()

        ps_o = [psC, psD, psE, psF]
        ps_sc = [psA, psB, psG, psA]
        exuse = {}
        scuse = {}
        tailuse = {}
        diaguse = {}
        lastago = None
        for e_ in range(4):
            L = cfg.L[e_]
            NT = L - 4
            dpar = e_ % 2
            Lp = cfg.L[e_ - 1] if e_ >= 1 else 0
            qdep = (f"k_sc_{e_ - 1}_{NKV - 1}_{Lp - 1}_{3}",) if e_ >= 1 else ()
            pp.dma("sync", "bq", key=f"k_bqn_{e_}", after=qdep,
                   fn=lambda e, e_=e_: e.dma_start(
                       out=qn_sb[:].rearrange("p (h t) -> p h t", h=NH),
                       in_=qn_scr[e_].rearrange("h p t -> p h t")))
            pp.dma("sync", "bq", key=f"k_bqr_{e_}",
                   fn=lambda e, e_=e_: e.dma_start(
                       out=qr_sb[:].rearrange("r (h t) -> r h t", h=NH),
                       in_=qr_scr[e_].rearrange("h r t -> r h t")))
            agdep = ((lastago,) if lastago else ()) + \
                ((f"k_agf_{e_ - 1}",) if e_ >= 1 else ())
            pp.dma("pool", "agf", key=f"k_agf_{e_}", after=agdep,
                   fn=lambda e, e_=e_: e.dma_start(
                       out=aug_dram[:],
                       in_=i_augv[DynSlice(s4_sv[e_], 1), 0:TMAX * 128]))
            ddep = (diaguse[dpar],) if dpar in diaguse else ()
            pp.dma("sync", f"dg{dpar}", key=f"k_dg_{e_}", after=ddep,
                   fn=lambda e, e_=e_, dpar=dpar: e.dma_start(
                       out=b_diag[dpar][:].rearrange("p (cg w) -> p cg w", w=384),
                       in_=kvz_cg[DynSlice(s16_sv[e_], 16)]
                       .rearrange("cg p w -> p cg w")))
            for g in range(NKV):
                tpar = (NKV * e_ + g) % 2
                if NT > 0:
                    tdep = (tailuse[tpar],) if tpar in tailuse else ()
                    pp.dma("sync", f"tl{tpar}", key=f"k_tl_{e_}_{g}", after=tdep,
                           fn=lambda e, g=g, tpar=tpar, NT=NT: e.dma_start(
                               out=b_tail[tpar][:, 0:NT * 384]
                               .rearrange("p (c w) -> p c w", w=384),
                               in_=kvz_scr[0:NT, g].rearrange("c p w -> p c w")))
                    pp.dma("pool", f"ago{tpar}", key=f"k_ago_{e_}_{g}",
                           after=(f"k_tl_{e_}_{g}", f"k_agf_{e_}"),
                           fn=lambda e, tpar=tpar, NT=NT: e.dma_start(
                               out=b_tail[tpar][64:65, 0:NT * 384]
                               .rearrange("one (c w) -> one c w", w=384)[:, :, 128:256],
                               in_=aug_dram[0:1, 0:NT * 128]
                               .rearrange("one (c w) -> one c w", w=128)))
                    lastago = f"k_ago_{e_}_{g}"
                for i in range(L):
                    if i < 4:
                        kvb = b_diag[dpar]
                        base = (4 * i + g) * 384
                        kvdep = f"k_dg_{e_}"
                    else:
                        kvb = b_tail[tpar]
                        base = (i - 4) * 384
                        kvdep = f"k_ago_{e_}_{g}"
                    epar = i % 2
                    for hh in range(4):
                        h = 4 * g + hh
                        sb_i = [0, 1, 2, 0][hh]
                        sc = ps_sc[sb_i]
                        scdep = scuse.get(sb_i)
                        pp.mm(lambda e, h=h, sc=sc, kvb=kvb, base=base: e.matmul(
                            sc[:], kvb[:, base:base + 128],
                            qn_sb[:, 512 * h:512 * (h + 1)],
                            start=True, stop=False), key=None,
                            after=(kvdep, f"k_bqn_{e_}", f"k_bqr_{e_}", scdep))
                        pp.mm(lambda e, h=h, sc=sc, kvb=kvb, base=base: e.matmul(
                            sc[:], kvb[0:65, base + 128:base + 256],
                            qr_sb[:, 512 * h:512 * (h + 1)],
                            start=False, stop=True), key=f"k_sc_{e_}_{g}_{i}_{hh}")
                        exdep = exuse.get((epar, hh))
                        pp.act(lambda e, hh=hh, sc=sc, epar=epar: e.activation(
                            ex_sb[epar][:, 512 * hh:512 * (hh + 1)], sc[:], AF.Exp),
                            key=f"k_exp_{e_}_{g}_{i}_{hh}",
                            after=(f"k_sc_{e_}_{g}_{i}_{hh}", exdep))
                        scuse[sb_i] = f"k_exp_{e_}_{g}_{i}_{hh}"
                    mkey = None
                    if i < 4:
                        mkey = f"k_msk_{e_}_{g}_{i}"
                        mi = i
                        pp.dve(lambda e, epar=epar, mi=mi: e.tensor_tensor(
                            ex_sb[epar][:].rearrange("p (h t) -> p h t", h=4),
                            ex_sb[epar][:].rearrange("p (h t) -> p h t", h=4),
                            c_masks[:, 512 * mi:512 * (mi + 1)]
                            .rearrange("p (o t) -> p o t", o=1).broadcast_to([128, 4, 512]),
                            ALU.mult), key=mkey,
                            after=tuple(f"k_exp_{e_}_{g}_{i}_{hh}" for hh in range(4)))
                    for hh in range(4):
                        dep = (mkey,) if mkey else (f"k_exp_{e_}_{g}_{i}_{hh}",)
                        if i == 0:
                            dep = dep + ((f"k_otf_{e_}_{g - 1}_{hh}",) if g >= 1 else
                                         ((f"k_otf_{e_ - 1}_{NKV - 1}_{hh}",) if e_ >= 1 else ()))
                        pp.mm(lambda e, hh=hh, kvb=kvb, base=base, epar=epar, i=i, L=L: e.matmul(
                            ps_o[hh][:], kvb[:, base + 256:base + 384],
                            ex_sb[epar][:, 512 * hh:512 * (hh + 1)],
                            start=(i == 0), stop=(i == L - 1)),
                            key=f"k_av_{e_}_{g}_{i}_{hh}", after=dep)
                        ddep2 = (mkey,) if mkey else (f"k_exp_{e_}_{g}_{i}_{hh}",)
                        if i == 0 and hh == 0:
                            ddep2 = ddep2 + ((f"k_dn4_{e_}_{g - 1}_{3}",) if g >= 1 else
                                             ((f"k_dn4_{e_ - 1}_{NKV - 1}_{3}",) if e_ >= 1 else ()))
                        if i == 0:
                            pp.dve(lambda e, hh=hh, epar=epar: e.tensor_copy(
                                exa_sb[:, 512 * hh:512 * (hh + 1)],
                                ex_sb[epar][:, 512 * hh:512 * (hh + 1)]),
                                key=f"k_avd_{e_}_{g}_{i}_{hh}", after=ddep2)
                        else:
                            pp.dve(lambda e, hh=hh, epar=epar: e.tensor_add(
                                exa_sb[:, 512 * hh:512 * (hh + 1)],
                                exa_sb[:, 512 * hh:512 * (hh + 1)],
                                ex_sb[epar][:, 512 * hh:512 * (hh + 1)]),
                                key=f"k_avd_{e_}_{g}_{i}_{hh}", after=ddep2)
                        exuse[(epar, hh)] = f"k_avd_{e_}_{g}_{i}_{hh}"
                if NT > 0:
                    tailuse[tpar] = f"k_av_{e_}_{g}_{L - 1}_{3}"
                for hh in range(4):
                    pp.mm(lambda e, hh=hh: e.matmul(
                        psG[0:4, :], c_ind4f[:, 4 * hh:4 * (hh + 1)],
                        exa_sb[:, 512 * hh:512 * (hh + 1)],
                        start=(hh == 0), stop=(hh == 3)),
                        key=f"k_dn4_{e_}_{g}_{hh}",
                        after=(f"k_avd_{e_}_{g}_{L - 1}_{hh}",) +
                        ((f"k_exp_{e_}_{g}_{L - 1}_{2}",) if hh == 0 else ()) +
                        ((f"k_rcp_{e_}_{g - 1}",) if hh == 0 and g >= 1 else
                         ((f"k_rcp_{e_ - 1}_{NKV - 1}",) if hh == 0 and e_ >= 1 else ())))
                pp.dve(lambda e: e.reciprocal(rcp_sb[:], psG[0:4, :]),
                       key=f"k_rcp_{e_}_{g}", after=(f"k_dn4_{e_}_{g}_{3}",))
                scuse[2] = f"k_rcp_{e_}_{g}"
                for hh in range(4):
                    h = 4 * g + hh
                    bdep = (f"k_rcp_{e_}_{g}",)
                    if hh >= 1:
                        bdep = bdep + (f"k_otf_{e_}_{g}_{hh - 1}",)
                    elif g >= 1:
                        bdep = bdep + (f"k_otf_{e_}_{g - 1}_{3}",)
                    elif e_ >= 1:
                        bdep = bdep + (f"k_otf_{e_ - 1}_{NKV - 1}_{3}",)
                    pp.mm(lambda e, hh=hh: e.matmul(
                        psH[:], c_indrow[:, 128 * hh:128 * (hh + 1)], rcp_sb[:],
                        start=True, stop=True), key=f"k_bc_{e_}_{g}_{hh}", after=bdep)
                    op2 = hh % 2
                    odep = ()
                    if hh >= 2:
                        odep = (f"k_otw_{e_}_{g}_{hh - 2}",)
                    elif g >= 1:
                        odep = (f"k_otw_{e_}_{g - 1}_{hh + 2}",)
                    elif e_ >= 1:
                        odep = (f"k_otw_{e_ - 1}_{NKV - 1}_{hh + 2}",)
                    bedep = (f"k_otf_{e_}_{g}_{hh - 1}",) if hh >= 1 else \
                        ((f"k_otf_{e_}_{g - 1}_{3}",) if g >= 1 else
                         ((f"k_otf_{e_ - 1}_{NKV - 1}_{3}",) if e_ >= 1 else ()))
                    pp.act(lambda e: e.activation(bce_sb[:], psH[:], AF.Copy),
                           key=f"k_bce_{e_}_{g}_{hh}",
                           after=(f"k_bc_{e_}_{g}_{hh}",) + bedep)
                    pp.dve(lambda e, hh=hh, op2=op2: e.tensor_mul(
                        otf_sb[op2][:], ps_o[hh][:], bce_sb[:]),
                        key=f"k_otf_{e_}_{g}_{hh}",
                        after=(f"k_bce_{e_}_{g}_{hh}", f"k_av_{e_}_{g}_{L - 1}_{hh}") + odep)
                    pp.dma("pool", f"otw{op2}", key=f"k_otw_{e_}_{g}_{hh}",
                           after=(f"k_otf_{e_}_{g}_{hh}",),
                           fn=lambda e, e_=e_, h=h, op2=op2: e.dma_start(
                               out=ot_scr[h, :, 512 * e_:512 * (e_ + 1)],
                               in_=otf_sb[op2][:]))
            diaguse[dpar] = f"k_av_{e_}_{NKV - 1}_{3}_{3}"
            pp.alias(f"k_bdone_{e_}", f"k_otw_{e_}_{NKV - 1}_{3}")
        pp.barrier()

        pp.dma("pool", "wol", key="k_woL", fn=lambda e: e.dma_start(
            out=wo_sb[:].rearrange("p (c d) -> p c d", c=NH),
            in_=wo.rearrange("(c p) d -> p c d", p=128)))
        pp.dma("sync", "otx", key="k_otxL", fn=lambda e: e.dma_start(
            out=otx_sb[:].rearrange("p (h t) -> p h t", h=NH),
            in_=ot_scr.rearrange("h p t -> p h t")))
        for dcx in range(DC):
            pb = [psA, psB, psC, psD] if dcx % 2 == 0 else [psE, psF, psG, psH]
            xpar = dcx % 2
            pp.dma("sync", f"xoc{xpar}", key=f"k_cx_{dcx}",
                   after=(f"k_x2w_{dcx - 2}",) if dcx >= 2 else (),
                   fn=lambda e, dcx=dcx, xpar=xpar: e.dma_start(
                       out=xoc[xpar][:], in_=xTo[128 * dcx:128 * (dcx + 1), :]))
            for hc in range(NH):
                dep0 = ()
                if hc == 0:
                    dep0 = ("k_woL", "k_otxL")
                    if dcx >= 2:
                        dep0 = dep0 + (f"k_x2a_{dcx - 2}_{3}",)
                for cb in range(4):
                    pp.mm(lambda e, dcx=dcx, hc=hc, cb=cb, pb=pb: e.matmul(
                        pb[cb][:],
                        wo_sb[:, D * hc + 128 * dcx:D * hc + 128 * dcx + 128],
                        otx_sb[:, OWN * hc + 512 * cb:OWN * hc + 512 * (cb + 1)],
                        start=(hc == 0), stop=(hc == NH - 1)),
                        key=f"k_wo_{dcx}_{cb}" if hc == NH - 1 else None,
                        after=dep0 if (hc == 0 and cb == 0) else ())
            for cb in range(4):
                pp.dve(lambda e, cb=cb, pb=pb, xpar=xpar: e.tensor_add(
                    x2e[:, 512 * cb:512 * (cb + 1)], pb[cb][:],
                    xoc[xpar][:, 512 * cb:512 * (cb + 1)]),
                    key=f"k_x2a_{dcx}_{cb}",
                    after=(f"k_wo_{dcx}_{cb}", f"k_cx_{dcx}") +
                    ((f"k_x2w_{dcx - 1}",) if cb == 0 and dcx >= 1 else ()))
            pp.dma("pool", "x2w", key=f"k_x2w_{dcx}",
                   after=tuple(f"k_x2a_{dcx}_{cb}" for cb in range(4)),
                   fn=lambda e, dcx=dcx: e.dma_start(out=x2_scr[dcx], in_=x2e[:]))
        pp.barrier()

        for dcx in range(DC):
            xp2 = dcx % 2
            pp.dma("pool", f"x2b{xp2}", key=f"k_x2b_{dcx}",
                   after=(f"k_st1_{dcx - 2}", f"k_st2_{dcx - 2}") if dcx >= 2 else (),
                   fn=lambda e, dcx=dcx, xp2=xp2: e.dma_start(
                       out=x2b[xp2][:], in_=x2_scr[dcx]))
            pp.act(lambda e, xp2=xp2: e.activation(sq2[xp2][:], x2b[xp2][:], AF.Square),
                   key=f"k_sq2_{dcx}", after=(f"k_x2b_{dcx}",) +
                   ((f"k_st2_{dcx - 2}",) if dcx >= 2 else ()))
            for cb in range(4):
                pp.mm(lambda e, cb=cb, xp2=xp2: e.matmul(
                    ps[cb][0:1, :], c_ones[:], x2b[xp2][:, 512 * cb:512 * (cb + 1)],
                    start=(dcx == 0), stop=(dcx == DC - 1)),
                    key=f"k_st1_{dcx}" if cb == 3 else None,
                    after=(f"k_x2b_{dcx}",) if cb == 0 else ())
                pp.mm(lambda e, cb=cb, xp2=xp2: e.matmul(
                    ps[4 + cb][0:1, :], c_ones[:], sq2[xp2][:, 512 * cb:512 * (cb + 1)],
                    start=(dcx == 0), stop=(dcx == DC - 1)),
                    key=f"k_st2_{dcx}" if cb == 3 else None,
                    after=(f"k_sq2_{dcx}",) if cb == 0 else ())
        for cb in range(4):
            pp.act(lambda e, cb=cb: e.activation(
                rows2[0:1, 512 * cb:512 * (cb + 1)], ps[cb][0:1, :], AF.Copy, scale=1.0 / D),
                key=f"k_r2a_{cb}", after=(f"k_st1_{DC - 1}", f"k_st2_{DC - 1}"))
            pp.act(lambda e, cb=cb: e.activation(
                rows2b[0:1, 512 * cb:512 * (cb + 1)], ps[4 + cb][0:1, :], AF.Copy, scale=1.0 / D),
                key=f"k_r2b_{cb}", after=(f"k_st2_{DC - 1}",))
        pp.act(lambda e: e.activation(rows2c[:], rows2[:], AF.Square),
               key="k_r2c", after=tuple(f"k_r2a_{cb}" for cb in range(4)))
        pp.dve(lambda e: e.tensor_sub(rows2b[:], rows2b[:], rows2c[:]),
               key="k_var2", after=("k_r2c",) + tuple(f"k_r2b_{cb}" for cb in range(4)))
        pp.dve(lambda e: e.tensor_scalar(rows2b[:], rows2b[:], EPS, None, ALU.add),
               key="k_var2e", after=("k_var2",))
        pp.act(lambda e: e.activation(rows2b[:], rows2b[:], AF.Ln),
               key="k_lnv2", after=("k_var2e",))
        pp.act(lambda e: e.activation(rows2b[:], rows2b[:], AF.Exp, scale=-0.5),
               key="k_istd2", after=("k_lnv2",))
        pp.dve(lambda e: e.tensor_mul(rows2c[:], rows2[:], rows2b[:]),
               key="k_ms2", after=("k_istd2",))
        for cb in range(4):
            pp.mm(lambda e, cb=cb: e.matmul(
                ps[cb][:], c_onesrow[:], rows2b[0:1, 512 * cb:512 * (cb + 1)],
                start=True, stop=True), key=f"k_b2s_{cb}",
                after=("k_istd2", f"k_r2a_{cb}"))
            pp.act(lambda e, cb=cb: e.activation(
                S2b[:, 512 * cb:512 * (cb + 1)], ps[cb][:], AF.Copy),
                key=f"k_b2se_{cb}", after=(f"k_b2s_{cb}",))
            pp.mm(lambda e, cb=cb: e.matmul(
                ps[4 + cb][:], c_onesrow[:], rows2c[0:1, 512 * cb:512 * (cb + 1)],
                start=True, stop=True), key=f"k_b2m_{cb}",
                after=("k_ms2", f"k_r2b_{cb}"))
            pp.act(lambda e, cb=cb: e.activation(
                M2b[:, 512 * cb:512 * (cb + 1)], ps[4 + cb][:], AF.Copy),
                key=f"k_b2me_{cb}", after=(f"k_b2m_{cb}",))
        for dcx in range(DC):
            xp2 = dcx % 2
            pp.dma("sync", f"x2f{xp2}", key=f"k_x2f_{dcx}",
                   after=(f"k_xn2_{dcx - 2}",) if dcx >= 2 else (),
                   fn=lambda e, dcx=dcx, xp2=xp2: e.dma_start(
                       out=x2f[xp2][:], in_=x2_scr[dcx]))
            pp.dve(lambda e, xp2=xp2: e.tensor_mul(silu[:], x2f[xp2][:], S2b[:]),
                   key=f"k_xn2m_{dcx}", after=(f"k_x2f_{dcx}", "k_b2se_3") +
                   ((f"k_xn2_{dcx - 1}",) if dcx >= 1 else ()))
            pp.dve(lambda e, dcx=dcx: e.tensor_sub(
                xn2[:, OWN * dcx:OWN * (dcx + 1)], silu[:], M2b[:]),
                key=f"k_xn2_{dcx}", after=(f"k_xn2m_{dcx}", "k_b2me_3"))
        for hc in range(HC):
            wpar = hc % 2
            wdep = (f"k_up1_{hc - 2}",) if hc >= 2 else ()
            pp.dma("pool", f"w1s{wpar}", key=f"k_w1L_{hc}", after=wdep,
                   fn=lambda e, hc=hc, wpar=wpar: e.dma_start(
                       out=w1s[wpar][:].rearrange("p (c f) -> p c f", c=DC),
                       in_=w1t[:, 128 * hc:128 * (hc + 1)].rearrange("(c p) f -> p c f", p=128)))
            pp.dma("pool", f"w3s{wpar}", key=f"k_w3L_{hc}", after=wdep,
                   fn=lambda e, hc=hc, wpar=wpar: e.dma_start(
                       out=w3s[wpar][:].rearrange("p (c f) -> p c f", c=DC),
                       in_=w3t[:, 128 * hc:128 * (hc + 1)].rearrange("(c p) f -> p c f", p=128)))
            for dcx in range(DC):
                dep1 = (f"k_xn2_{dcx}",)
                if dcx == 0:
                    dep1 = dep1 + (f"k_w1L_{hc}",)
                    if hc >= 2:
                        dep1 = dep1 + (f"k_sil_{hc - 2}_{3}",)
                for cb in range(4):
                    pp.mm(lambda e, dcx=dcx, cb=cb, wpar=wpar: e.matmul(
                        ps[cb][:], w1s[wpar][:, 128 * dcx:128 * (dcx + 1)],
                        xn2[:, OWN * dcx + 512 * cb:OWN * dcx + 512 * (cb + 1)],
                        start=(dcx == 0), stop=(dcx == DC - 1)),
                        key=f"k_up1_{hc}" if (dcx == DC - 1 and cb == 3) else None,
                        after=dep1 if cb == 0 else ())
                dep3 = (f"k_xn2_{dcx}",)
                if dcx == 0:
                    dep3 = dep3 + (f"k_w3L_{hc}",)
                    if hc >= 2:
                        dep3 = dep3 + (f"k_hid_{hc - 2}_{3}",)
                for cb in range(4):
                    pp.mm(lambda e, dcx=dcx, cb=cb, wpar=wpar: e.matmul(
                        ps[4 + cb][:], w3s[wpar][:, 128 * dcx:128 * (dcx + 1)],
                        xn2[:, OWN * dcx + 512 * cb:OWN * dcx + 512 * (cb + 1)],
                        start=(dcx == 0), stop=(dcx == DC - 1)),
                        key=f"k_up3_{hc}" if (dcx == DC - 1 and cb == 3) else None,
                        after=dep3 if cb == 0 else ())
            for cb in range(4):
                pp.act(lambda e, cb=cb: e.activation(
                    silu[:, 512 * cb:512 * (cb + 1)], ps[cb][:], AF.Silu),
                    key=f"k_sil_{hc}_{cb}", after=(f"k_up1_{hc}",) +
                    ((f"k_hid_{hc - 1}_{3}",) if hc >= 1 else ()))
                pp.dve(lambda e, cb=cb: e.tensor_mul(
                    hidsb[:, 512 * cb:512 * (cb + 1)], ps[4 + cb][:],
                    silu[:, 512 * cb:512 * (cb + 1)]),
                    key=f"k_hid_{hc}_{cb}",
                    after=(f"k_sil_{hc}_{cb}", f"k_up3_{hc}") +
                    ((f"k_hw_{hc - 1}",) if cb == 0 and hc >= 1 else ()))
            pp.dma("pool", "hw", key=f"k_hw_{hc}",
                   after=tuple(f"k_hid_{hc}_{cb}" for cb in range(4)),
                   fn=lambda e, hc=hc: e.dma_start(out=hid_scr[hc], in_=hidsb[:]))
        pp.barrier()

        for tq in range(4):
            pp.dma("sync", "hq", key=f"k_hq_{tq}",
                   after=(f"k_ya_{tq - 1}_{7}_{1}",) if tq >= 1 else (),
                   fn=lambda e, tq=tq: e.dma_start(
                       out=hidq[:].rearrange("p (c t) -> p c t", c=HC),
                       in_=hid_scr[:, :, 512 * tq:512 * (tq + 1)].rearrange("c p t -> p c t")))
            for dp in range(8):
                wpar = dp % 2
                pbank = [psA, psB] if dp % 2 == 0 else [psC, psD]
                wdep = (f"k_dn_{tq}_{dp - 2}_{1}",) if dp >= 2 else \
                    ((f"k_dn_{tq - 1}_{6 + dp}_{1}",) if tq >= 1 else ())
                pp.dma("pool", f"w2s{wpar}", key=f"k_w2L_{tq}_{dp}", after=wdep,
                       fn=lambda e, dp=dp, wpar=wpar: e.dma_start(
                           out=w2hs[wpar][:].rearrange("p (c w) -> p c w", c=HC),
                           in_=w2h_scr[:, :, dp, :].rearrange("c p w -> p c w")))
                for hc in range(HC):
                    dep0 = ()
                    if hc == 0:
                        dep0 = (f"k_w2L_{tq}_{dp}", f"k_hq_{tq}")
                        if dp >= 2:
                            dep0 = dep0 + (f"k_ya_{tq}_{dp - 2}_{1}",)
                        elif tq >= 1:
                            dep0 = dep0 + (f"k_ya_{tq - 1}_{6 + dp}_{1}",)
                    for j in range(2):
                        pp.mm(lambda e, hc=hc, j=j, wpar=wpar, pbank=pbank: e.matmul(
                            pbank[j][:],
                            w2hs[wpar][:, 256 * hc + 128 * j:256 * hc + 128 * (j + 1)],
                            hidq[:, 512 * hc:512 * (hc + 1)],
                            start=(hc == 0), stop=(hc == HC - 1)),
                            key=f"k_dn_{tq}_{dp}_{j}" if hc == HC - 1 else None,
                            after=dep0 if j == 0 else ())
                for j in range(2):
                    xp = j
                    pp.dma("sync", f"x2c{xp}", key=f"k_x2c_{tq}_{dp}_{j}",
                           after=(f"k_ya_{tq}_{dp - 1}_{j}",) if dp >= 1 else
                           ((f"k_ya_{tq - 1}_{7}_{j}",) if tq >= 1 else ()),
                           fn=lambda e, tq=tq, dp=dp, j=j, xp=xp: e.dma_start(
                               out=x2c[xp][:],
                               in_=x2_scr[2 * dp + j][:, 512 * tq:512 * (tq + 1)]))
                    pp.dve(lambda e, j=j, pbank=pbank, xp=xp: e.tensor_add(
                        y_sb[j][:], pbank[j][:], x2c[xp][:]),
                        key=f"k_ya_{tq}_{dp}_{j}",
                        after=(f"k_dn_{tq}_{dp}_{j}", f"k_x2c_{tq}_{dp}_{j}") +
                        ((f"k_yw_{tq}_{dp - 1}_{0}", f"k_yw_{tq}_{dp - 1}_{1}") if dp >= 1 else
                         ((f"k_yw_{tq - 1}_{7}_{0}", f"k_yw_{tq - 1}_{7}_{1}") if tq >= 1 else ())))
                    pp.dma("sync", "yw", key=f"k_yw_{tq}_{dp}_{j}",
                           after=(f"k_ya_{tq}_{dp}_{j}",),
                           fn=lambda e, tq=tq, dp=dp, j=j: e.dma_start(
                               out=yT[128 * (2 * dp + j):128 * (2 * dp + j + 1),
                                      512 * tq:512 * (tq + 1)],
                               in_=y_sb[j][:]))
        pp.barrier()

    def emit_rope(pp, tag, deps, out_even, out_odd, use_rout, pb=0):
        rr, rs, t1, t2 = rraw2[pb], rswp2[pb], rt1_2[pb], rt2_2[pb]
        for k4, (a, b2) in enumerate(((0, 32), (32, 0), (64, 96), (96, 64))):
            pp.dma("sync", f"swp{pb}", key=f"k_swp_{tag}_{k4}", after=deps,
                   fn=lambda e, a=a, b2=b2: e.dma_start(
                       out=rs[a:a + 32, :], in_=rr[b2:b2 + 32, :]))
        swpk = tuple(f"k_swp_{tag}_{k4}" for k4 in range(4))
        pp.dve(lambda e: e.tensor_mul(t1[:], rr[:], cosD[:]),
               key=f"k_rp1_{tag}", after=deps)
        pp.dve(lambda e: e.tensor_mul(t2[:], rs[:], sinD[:]),
               key=f"k_rp2_{tag}", after=swpk + deps)
        pp.dve(lambda e: e.tensor_add(out_even, t1[0:64, :], t2[0:64, :]),
               key=f"k_rp3_{tag}", after=(f"k_rp1_{tag}", f"k_rp2_{tag}"))
        pp.dve(lambda e: e.tensor_add(out_odd, t1[64:128, :], t2[64:128, :]),
               key=f"k_rope_{tag}", after=(f"k_rp3_{tag}",))

    with nc.Block() as blk:
        blk.sync(lambda eng: (p.start("sync", eng), walk(p)))
        blk.gpsimd(lambda eng: (p.start("pool", eng), walk(p)))
        blk.tensor(lambda eng: (p.start("pe", eng), walk(p)))
        blk.scalar(lambda eng: (p.start("act", eng), walk(p)))
        blk.vector(lambda eng: (p.start("dve", eng), walk(p)))

    return nc



def _prep(inputs, cfg):
    x = np.ascontiguousarray(inputs["L0_attn_hidden_states"], np.float32)
    B = x.shape[0]
    xT = [np.ascontiguousarray(x[b].T) for b in range(B)]
    wqkv_p = np.ascontiguousarray(inputs["L0_attn_Wqkv"][:, qkv_perm()], np.float32)
    wkup_p = np.ascontiguousarray(inputs["L0_attn_Wk_up"][:, kup_perm()], np.float32)
    wvup = np.ascontiguousarray(inputs["L0_attn_Wv_up"], np.float32)
    wo = np.ascontiguousarray(inputs["L0_attn_Wo"], np.float32)
    w1t = np.ascontiguousarray(inputs["L0_ffn_W1"].T, np.float32)
    w3t = np.ascontiguousarray(inputs["L0_ffn_W3"].T, np.float32)
    w2t = np.ascontiguousarray(inputs["L0_ffn_W2"].T, np.float32)
    in_maps = []
    owns = []
    for c in range(8):
        b = c // 4
        sl = slabs_of(c, cfg)
        ids = np.concatenate([np.arange(512 * s, 512 * (s + 1)) for s in sl])
        owns.append((b, ids))
        in_maps.append({
            "xTb": xT[b],
            "xTo": np.ascontiguousarray(xT[b][:, ids]),
            "posin": ids.astype(np.float32)[None, :],
            "wqkv": wqkv_p, "wkup": wkup_p, "wvup": wvup, "wo": wo,
            "w1t": w1t, "w3t": w3t, "w2t": w2t,
        })
    return in_maps, owns, x.shape


def kernel(**inputs):
    cfg = Cfg(inputs["L0_attn_hidden_states"].shape[1])
    in_maps, owns, xshape = _prep(inputs, cfg)
    nc = build_program(cfg)
    res = run_bass_kernel_spmd(nc, in_maps, core_ids=list(range(8)))
    y = np.zeros(xshape, np.float32)
    for c in range(8):
        b, ids = owns[c]
        y[b, ids, :] = np.asarray(res.results[c]["yT"], np.float32).T
    return y


# version 17 (final)
# speedup vs baseline: 1.0204x; 1.0204x over previous
import math

import numpy as np
import ml_dtypes

import concourse.bass as bass
import concourse.mybir as mybir
from concourse.bass_utils import run_bass_kernel_spmd
from concourse.expressions import RuntimeValue
from concourse.bass_types import DynSlice

F32 = mybir.dt.float32
BF16 = mybir.dt.bfloat16
AF = mybir.ActivationFunctionType
ALU = mybir.AluOpType
BF = ml_dtypes.bfloat16
AXX = mybir.AxisListType.X

D = 2048
DC = 16
NH = 16
NKV = 4
LORA = 512
CC = 4
DN = 128
DR = 64
DQK = 192
DV = 128
HID = 8192
HC = 64
OWN = 2048
SCALE = float(np.float32(1.0 / math.sqrt(DQK)))
EPS = 1e-5
BIGAUG = 1.0e5
FWHT_SCALE = float(np.float32(1.0 / math.sqrt(HID)))

TWO_PI = 6.283185307179586
CW1 = float(np.float32(6.28125))
CW2 = float(np.float32(TWO_PI - 6.28125))
CW3 = float(TWO_PI - CW1 - float(np.float32(TWO_PI - 6.28125)))
MAGIC = float(np.float32(1.5 * 2 ** 23))
INVF_SCALE = float(np.float32(-math.log(10000.0) / 32.0))

H128NP = np.array([[(-1) ** bin(i & j).count("1") for j in range(128)]
                   for i in range(128)], np.float32)

ENGS = ("sync", "pool", "pe", "act", "dve")


class Cfg:
    def __init__(self, S):
        self.S = S
        self.NSLAB = S // 512
        self.NCH = S // 128
        self.SPC = self.NSLAB // 4
        self.FAKE = self.NCH
        self.L = [4 * self.SPC * (e + 1) for e in range(4)]


def slabs_of(core, cfg):
    r = core % 4
    if cfg.SPC > 1:
        return [cfg.SPC * e + ((r + e) % cfg.SPC) for e in range(4)]
    return [e for e in range(4)]


def qkv_perm():
    idx = []
    for h in range(NH):
        idx += [h * DQK + d for d in range(DN)]
    for p in range(NH // 2):
        he, ho = 2 * p, 2 * p + 1
        idx += [he * DQK + DN + d for d in range(32)]
        idx += [he * DQK + DN + 32 + d for d in range(32)]
        idx += [ho * DQK + DN + d for d in range(32)]
        idx += [ho * DQK + DN + 32 + d for d in range(32)]
    idx += list(range(NH * DQK, NH * DQK + LORA))
    return np.array(idx)


def kup_perm():
    idx = []
    for g in range(NKV):
        idx += [g * DQK + d for d in range(DN)]
    for p in range(NKV // 2):
        ge, go = 2 * p, 2 * p + 1
        idx += [ge * DQK + DN + d for d in range(32)]
        idx += [ge * DQK + DN + 32 + d for d in range(32)]
        idx += [go * DQK + DN + d for d in range(32)]
        idx += [go * DQK + DN + 32 + d for d in range(32)]
    return np.array(idx)



class P:

    def __init__(self, nc):
        self.nc = nc
        self.tick = {e: nc.alloc_semaphore("tk_" + e) for e in ENGS}
        self.dsem = {}

    def start(self, cur, eng):
        self.cur = cur
        self.eng = eng
        self.cnt = {e: 0 for e in ENGS}
        self.dcnt = {}
        self.tok = {}
        self.wm = {}

    def _sem(self, name):
        if name not in self.dsem:
            self.dsem[name] = self.nc.alloc_semaphore("d_" + name)
        return self.dsem[name]

    def after(self, keys):
        best = {}
        for k in keys:
            if k is None:
                continue
            kind, name, val = self.tok[k]
            wmk = (kind, name)
            if val > best.get(wmk, 0):
                best[wmk] = val
        for (kind, name), val in best.items():
            if self.wm.get((kind, name), 0) >= val:
                continue
            self.wm[(kind, name)] = val
            h = self.tick[name] if kind == "t" else self._sem(name)
            self.eng.wait_ge(h, val)

    AUTOCHAIN = ("dve", "pool", "act")

    def op(self, engname, fn, key=None, after=()):
        chain = engname in self.AUTOCHAIN
        if self.cur == engname:
            if chain and self.cnt[engname] > 0:
                k = ("t", engname)
                v = self.cnt[engname]
                if self.wm.get(k, 0) < v:
                    self.wm[k] = v
                    self.eng.wait_ge(self.tick[engname], v)
            self.after(after)
            inst = fn(self.eng)
        else:
            inst = None
        if key is not None or chain:
            self.cnt[engname] += 1
            if inst is not None:
                inst.then_inc(self.tick[engname], 1)
            if key is not None:
                self.tok[key] = ("t", engname, self.cnt[engname])

    def mm(self, fn, key=None, after=()):
        self.op("pe", fn, key, after)

    def act(self, fn, key=None, after=()):
        self.op("act", fn, key, after)

    def dve(self, fn, key=None, after=()):
        self.op("dve", fn, key, after)

    def pool(self, fn, key=None, after=()):
        self.op("pool", fn, key, after)

    def raw(self, engname, fn):
        if self.cur == engname:
            fn(self.eng)

    def dma(self, queue, semname, fn, key=None, after=()):
        self._sem(semname)
        if self.cur == queue:
            self.after(after)
            inst = fn(self.eng)
            inst.then_inc(self.dsem[semname], 16)
        self.dcnt[semname] = self.dcnt.get(semname, 0) + 16
        if key is not None:
            self.tok[key] = ("d", semname, self.dcnt[semname])

    def alias(self, key, srckey):
        self.tok[key] = self.tok[srckey]

    def barrier(self):
        for e in ENGS:
            if self.cnt[e] > 0 and e != self.cur:
                k = ("t", e)
                if self.wm.get(k, 0) < self.cnt[e]:
                    self.wm[k] = self.cnt[e]
                    self.eng.wait_ge(self.tick[e], self.cnt[e])
        for name, v in sorted(self.dcnt.items()):
            k = ("d", name)
            if v > 0 and self.wm.get(k, 0) < v:
                self.wm[k] = v
                self.eng.wait_ge(self.dsem[name], v)



def build_program(cfg: Cfg):
    nc = bass.Bass("TRN2", target_bir_lowering=False)
    S = cfg.S
    NCH = cfg.NCH
    FAKE = cfg.FAKE

    xTb = nc.dram_tensor("xTb", [D, S], F32, kind="ExternalInput").ap()
    xTo = nc.dram_tensor("xTo", [D, OWN], F32, kind="ExternalInput").ap()
    posin = nc.dram_tensor("posin", [1, OWN], F32, kind="ExternalInput").ap()
    wqkv = nc.dram_tensor("wqkv", [D, NH * DQK + LORA], F32, kind="ExternalInput").ap()
    wkup = nc.dram_tensor("wkup", [LORA, NKV * DQK], F32, kind="ExternalInput").ap()
    wvup = nc.dram_tensor("wvup", [LORA, NKV * DV], F32, kind="ExternalInput").ap()
    wo = nc.dram_tensor("wo", [NH * DV, D], F32, kind="ExternalInput").ap()
    w1t = nc.dram_tensor("w1t", [D, HID], F32, kind="ExternalInput").ap()
    w3t = nc.dram_tensor("w3t", [D, HID], F32, kind="ExternalInput").ap()
    w2t = nc.dram_tensor("w2t", [HID, D], F32, kind="ExternalInput").ap()
    yT = nc.dram_tensor("yT", [D, OWN], F32, kind="ExternalOutput").ap()

    def inl(arr, name):
        return nc.inline_tensor(arr, name=name).ap()

    i_ones = inl(np.ones((128, 1), BF), "c_ones")
    i_onesbf = inl(np.ones((1, 128), BF), "c_onesbf")
    i_onesrow = inl(np.ones((1, 128), np.float32), "c_onesrow")
    ind4np = np.zeros((128, 4 * 4), np.float32)
    for g in range(4):
        ind4np[:, 4 * g + g] = 1.0
    i_ind4 = inl(ind4np.astype(BF), "c_ind4")
    i_ind4f = inl(ind4np, "c_ind4f")
    indrownp = np.zeros((4, 4 * 128), np.float32)
    for g in range(4):
        indrownp[g, 128 * g:128 * (g + 1)] = 1.0
    i_indrow = inl(indrownp, "c_indrow")
    i16n = np.zeros((128, 16 * NH), np.float32)
    for h in range(NH):
        i16n[:, 16 * h + h] = 1.0
    i_i16n = inl(i16n.astype(BF), "c_i16n")
    i16r = np.zeros((128, 16 * (NH // 2)), np.float32)
    for p_ in range(NH // 2):
        i16r[0:64, 16 * p_ + 2 * p_] = 1.0
        i16r[64:128, 16 * p_ + 2 * p_ + 1] = 1.0
    i_i16r = inl(i16r.astype(BF), "c_i16r")
    i4r = np.zeros((128, 4 * (NKV // 2)), np.float32)
    for p_ in range(NKV // 2):
        i4r[0:64, 4 * p_ + 2 * p_] = 1.0
        i4r[64:128, 4 * p_ + 2 * p_ + 1] = 1.0
    i_i4r = inl(i4r.astype(BF), "c_i4r")
    i_exp4 = inl(np.repeat(np.eye(4, dtype=np.float32), 4, axis=1), "c_exp4")
    i_h128 = inl(H128NP.astype(BF), "c_h128")
    h64 = np.zeros((64, 64), np.float32)
    for i_ in range(64):
        for j_ in range(64):
            h64[i_, j_] = 1.0 if bin(i_ & j_).count("1") % 2 == 0 else -1.0
    i_h64 = inl(h64.astype(BF), "c_h64")
    msk = np.zeros((128, 4 * 512), np.float32)
    for k in range(4):
        msk[:, 512 * k:512 * (k + 1)] = (
            np.arange(512)[None, :] - np.arange(128)[:, None] >= 128 * k)
    i_masks = inl(msk.astype(BF), "c_masks")
    augv = np.ones((64, NCH * 128), np.float32)
    for s_ in range(cfg.NSLAB):
        for c_ in range(NCH):
            if c_ >= 4 * s_:
                augv[4 * s_:4 * s_ + 4, 128 * c_:128 * (c_ + 1)] = BIGAUG
    i_augv = inl(augv.astype(BF), "c_augv")

    kvz_scr = nc.dram_tensor("kvz_scr", [NCH, NKV, 128, 384], BF16, kind="Internal").ap()
    kvz_cg = kvz_scr.rearrange("c g p w -> (c g) p w")
    kn_scr = kvz_scr[:, :, :, 0:128]
    kr_scr = kvz_scr[:, :, 0:65, 128:256]
    v_scr = kvz_scr[:, :, :, 256:384]
    w2h_scr = nc.dram_tensor("w2h_scr", [HC, 128, 8, 256], BF16, kind="Internal").ap()
    waA_scr = nc.dram_tensor("waA_scr", [8, HC, 128, 256], BF16, kind="Internal").ap()
    aug_dram = nc.dram_tensor("aug_dram", [1, (cfg.L[3] - 4) * 128], BF16,
                              kind="Internal").ap()
    qn_scr = nc.dram_tensor("qn_scr", [4, NH, 128, 512], BF16, kind="Internal").ap()
    qr_scr = nc.dram_tensor("qr_scr", [4, NH, 65, 512], BF16, kind="Internal").ap()
    ot_scr = nc.dram_tensor("ot_scr", [NH, 128, OWN], BF16, kind="Internal").ap()
    x2_scr = nc.dram_tensor("x2_scr", [DC, 128, OWN], F32, kind="Internal").ap()
    hid_scr = nc.dram_tensor("hid_scr", [HC, 128, OWN], BF16, kind="Internal").ap()

    pid = nc.partition_id()
    s4_sv = []
    s16_sv = []
    for e in range(4):
        if cfg.SPC > 1:
            expr4 = 4 * (cfg.SPC * e) + 4 * ((pid % 4 + e) % cfg.SPC)
        else:
            expr4 = 4 * e + pid * 0
        s4_sv.append(nc.snap(expr4, min_val=0, max_val=4 * (cfg.NSLAB - 1)))
        s16_sv.append(nc.snap(4 * expr4, min_val=0, max_val=16 * (cfg.NSLAB - 1)))



    MANUAL_BYTES = 204800
    _rng = nc.bump_sbuf(MANUAL_BYTES, align_bytes=64)
    assert _rng is not None
    MBASE = _rng[0]
    sbat = nc.alloc_sbuf_tensor_at

    class Bump:
        def __init__(self, base):
            self.off = base

        def al(self, name, shape, dtype, align=64):
            self.off = (self.off + align - 1) // align * align
            t = sbat(name, shape, dtype, offset=MBASE + self.off)
            nbytes = int(np.prod(shape[1:])) * (2 if dtype == BF16 else 4)
            self.off += nbytes
            assert self.off <= MANUAL_BYTES, (name, self.off)
            return t

    perm = Bump(0)
    c_ones = perm.al("c_ones", [128, 1], BF16)
    c_onesbf = perm.al("c_onesbf", [1, 128], BF16)
    c_onesrow = perm.al("c_onesrow", [1, 128], F32)
    c_ind4 = perm.al("c_ind4", [128, 16], BF16)
    c_ind4f = perm.al("c_ind4f", [128, 16], F32)
    c_indrow = perm.al("c_indrow", [4, 512], F32)
    c_i16n = perm.al("c_i16n", [128, 256], BF16)
    c_i16r = perm.al("c_i16r", [128, 128], BF16)
    c_i4r = perm.al("c_i4r", [128, 8], BF16)
    c_exp4 = perm.al("c_exp4", [4, 16], F32)
    c_h128 = perm.al("c_h128", [128, 128], BF16)
    c_h64 = perm.al("c_h64", [64, 64], BF16)
    c_masks = perm.al("c_masks", [128, 2048], BF16)
    wsum = perm.al("wsum", [128, 4], F32)
    kmax = perm.al("kmax", [4, 1], F32)
    nkmax = perm.al("nkmax", [16, 1], F32)
    rowsA = perm.al("rowsA", [1, 512], F32)
    rowsB = perm.al("rowsB", [1, 512], F32)
    rowsC = perm.al("rowsC", [1, 512], F32)
    zt128 = perm.al("zt128", [128, 128], BF16)
    P0 = ((perm.off + 63) // 64 * 64)

    w_ = Bump(P0)
    wpr = [w_.al(f"wpr{i}", [128, HC * 256], BF16) for i in range(2)]
    wev = [w_.al(f"wev{i}", [128, 512], BF16) for i in range(2)]
    wBi = [w_.al(f"wBi{i}", [64, 16 * 256], BF16) for i in range(2)]
    wBo = [w_.al(f"wBo{i}", [128, 4 * 512], BF16) for i in range(2)]

    m_ = Bump(P0)
    mcos = m_.al("mcos", [32, S], F32)
    msin = m_.al("msin", [32, S], F32)
    ocos = m_.al("ocos", [32, OWN], F32)
    osin = m_.al("osin", [32, OWN], F32)
    P1 = ((m_.off + 63) // 64 * 64)

    t_ = Bump(P1)
    mt_iota = t_.al("mt_iota", [1, S], F32)
    mt_invf = t_.al("mt_invf", [1, 32], F32)
    tth = t_.al("tth", [32, S], F32)
    ttk = t_.al("ttk", [32, S], F32)

    a_ = Bump(P1)
    xs_sb = [a_.al(f"xs{i}", [128, DC * 512], BF16) for i in range(2)]
    csb = a_.al("csb", [128, CC * 512], BF16)
    cosD = a_.al("cosD", [128, 512], F32)
    sinD = a_.al("sinD", [128, 512], F32)
    rraw2 = [a_.al(f"rraw{i}", [128, 512], F32) for i in range(2)]
    rswp2 = [a_.al(f"rswp{i}", [128, 512], F32) for i in range(2)]
    rt1_s = a_.al("rt1", [128, 512], F32)
    rt2_s = a_.al("rt2", [128, 512], F32)
    rt1_2 = [rt1_s, rt1_s]
    rt2_2 = [rt2_s, rt2_s]
    rt1 = rt1_s
    rt2 = rt2_s
    rout2 = [a_.al(f"rout{i}", [128, 512], BF16) for i in range(2)]
    rraw = rraw2[0]
    sqs = a_.al("sqs", [128, 512], BF16)
    kev = a_.al("kev", [128, 512], BF16)
    sS = a_.al("sS", [128, 512], F32)
    sM = a_.al("sM", [128, 512], F32)
    xnf = a_.al("xnf", [128, 512], F32)
    PA = ((a_.off + 63) // 64 * 64)
    aw = Bump(PA)
    wc_sb = aw.al("wc_sb", [128, DC * 512], BF16)
    wk_sb = aw.al("wk_sb", [128, CC * NKV * DQK], BF16)
    wv_sb = aw.al("wv_sb", [128, CC * NKV * DV], BF16)
    sqx = aw.al("sqx", [128, 8 * 512], BF16)
    aq = Bump(PA)
    wqs = [aq.al(f"wqs{i}", [128, DC * 128], BF16) for i in range(2)]
    xno = aq.al("xno", [128, DC * 512], BF16)
    qaug = aq.al("qaug", [65, NH * 512], BF16)
    qn2e = aq.al("qn2e", [16, 512], F32)
    qev = aq.al("qev", [128, 512], BF16)

    b_ = Bump(P0)
    TMAX = cfg.L[3] - 4
    b_tail = [b_.al(f"b_tail{i}", [128, TMAX * 384], BF16) for i in range(2)]
    b_diag = [b_.al(f"b_diag{i}", [128, 16 * 384], BF16) for i in range(2)]

    qn_sb = b_.al("qn_sb", [128, NH * 512], BF16)
    qr_sb = b_.al("qr_sb", [65, NH * 512], BF16)
    ex_sb = [b_.al(f"ex_sb{i}", [128, 4 * 512], BF16) for i in range(2)]
    exa_sb = b_.al("exa_sb", [128, 4 * 512], F32)
    rcp_sb = b_.al("rcp_sb", [4, 512], F32)
    bce_sb = b_.al("bce_sb", [128, 512], F32)
    otf_sb = [b_.al(f"otf_sb{i}", [128, 512], BF16) for i in range(2)]

    c1 = Bump(P0)
    wo_sb = c1.al("wo_sb", [128, NH * 128], BF16)
    c1 = Bump(P0)
    wo_sb = sbat("wo_sb2", [128, NH * D], BF16, offset=MBASE + P0)
    otx_sb = sbat("otx_sb", [128, NH * OWN], BF16, offset=MBASE + P0 + NH * D * 2)
    _c1off = P0 + NH * D * 2 + NH * OWN * 2
    xoc = [sbat(f"xoc{i}", [128, OWN], F32, offset=MBASE + _c1off + i * OWN * 4) for i in range(2)]
    x2e = sbat("x2e", [128, OWN], F32, offset=MBASE + _c1off + 2 * OWN * 4)
    assert _c1off + 3 * OWN * 4 <= MANUAL_BYTES

    c2 = Bump(P0)
    xn2 = c2.al("xn2", [128, DC * OWN], BF16)
    S2b = c2.al("S2b", [128, OWN], F32)
    M2b = c2.al("M2b", [128, OWN], F32)
    rows2 = c2.al("rows2", [1, OWN], F32)
    rows2b = c2.al("rows2b", [1, OWN], F32)
    rows2c = c2.al("rows2c", [1, OWN], F32)
    x2b = [c2.al(f"x2b{i}", [128, OWN], BF16) for i in range(2)]
    sq2 = [c2.al(f"sq2{i}", [128, OWN], BF16) for i in range(2)]
    x2f = [c2.al(f"x2f{i}", [128, OWN], F32) for i in range(2)]
    w1s = [c2.al(f"w1s{i}", [128, DC * 128], BF16) for i in range(2)]
    w3s = [c2.al(f"w3s{i}", [128, DC * 128], BF16) for i in range(2)]
    silu = c2.al("silu", [128, OWN], F32)
    hidsb = c2.al("hidsb", [128, OWN], BF16)

    c3 = Bump(P0)
    hidq = c3.al("hidq", [128, HC * 512], BF16)
    w2hs = [c3.al(f"w2hs{i}", [128, HC * 256], BF16) for i in range(2)]
    x2c = [c3.al(f"x2c{i}", [128, 512], F32) for i in range(2)]
    y_sb = [c3.al(f"y_sb{i}", [128, 512], F32) for i in range(2)]

    pz = nc.alloc_psum_tensor
    ps = [pz(f"ps{i}", [128, 512], F32) for i in range(8)]
    psA, psB, psC, psD, psE, psF, psG, psH = ps

    p = P(nc)

    def walk(pp: P):
        cl = [(c_ones, i_ones), (c_onesbf, i_onesbf), (c_onesrow, i_onesrow),
              (c_ind4, i_ind4), (c_ind4f, i_ind4f), (c_indrow, i_indrow), (c_i16n, i_i16n),
              (c_i16r, i_i16r), (c_i4r, i_i4r), (c_exp4, i_exp4),
              (c_h128, i_h128), (c_h64, i_h64), (c_masks, i_masks)]
        for ci_, (dst, src) in enumerate(cl):
            pp.dma("sync", "lc", key="k_consts" if ci_ == len(cl) - 1 else None,
                   fn=lambda e, dst=dst, src=src: e.dma_start(out=dst[:], in_=src))
        pp.pool(lambda e: e.memset(zt128[:], 0.0), key="k_zt")
        for g in range(NKV):
            pp.dma("pool", "fk", key=f"k_pad_{g}", after=("k_zt",),
                   fn=lambda e, g=g: e.dma_start(
                       out=kvz_scr[:, g, 65:128, 128:256].rearrange("c r w -> r c w"),
                       in_=zt128[0:63, :].rearrange("r (c w) -> r c w", c=1)
                       .broadcast_to([63, NCH, 128])))
            pp.dma("pool", "fk", key=f"k_augr_{g}", after=("k_consts",),
                   fn=lambda e, g=g: e.dma_start(
                       out=kvz_scr[0:NCH, g, 64:65, 128:256].rearrange("c one w -> one c w"),
                       in_=c_onesbf[0:1, :].rearrange("one (c w) -> one c w", c=1)
                       .broadcast_to([1, NCH, 128])))

        for sl in range(10):
            if sl < 8:
                wp = sl % 2
                pp.dma("pool", f"w2in{wp}", key=f"k_w2in_{sl}",
                       after=(f"k_wmm_{sl - 2}_{31}",) if sl >= 2 else (),
                       fn=lambda e, sl=sl, wp=wp: e.dma_start(
                           out=wpr[wp][:].rearrange("p (c w) -> p c w", c=HC),
                           in_=w2t[:, 256 * sl:256 * (sl + 1)]
                           .rearrange("(c p) w -> p c w", p=128)))
                for half in range(HC // 2):
                    c0 = 2 * half
                    ep = half % 2
                    dep = [f"k_w2in_{sl}", "k_consts"]
                    if half >= 2 or sl >= 1:
                        dep.append(f"k_wev_{sl}_{half - 2}" if half >= 2 else
                                   f"k_wev_{sl - 1}_{HC // 2 - 2 + half}")
                    pw = psC if half % 2 == 0 else psD
                    pp.mm(lambda e, c0=c0, pw=pw, wp=wp: e.matmul(
                        pw[:], c_h128[:], wpr[wp][:, 256 * c0:256 * (c0 + 2)],
                        start=True, stop=True), key=f"k_wmm_{sl}_{half}", after=tuple(dep))
                    wodep = ()
                    if half >= 2:
                        wodep = (f"k_w2o_{sl}_{half - 2}",)
                    elif sl >= 1:
                        wodep = (f"k_w2o_{sl - 1}_{HC // 2 - 2 + half}",)
                    pp.act(lambda e, pw=pw, ep=ep: e.activation(
                        wev[ep][:], pw[:], AF.Copy, scale=FWHT_SCALE),
                        key=f"k_wev_{sl}_{half}", after=(f"k_wmm_{sl}_{half}",) + wodep)
                    pp.dma("sync", f"w2o{ep}", key=f"k_w2o_{sl}_{half}",
                           after=(f"k_wev_{sl}_{half}",),
                           fn=lambda e, sl=sl, c0=c0, ep=ep: e.dma_start(
                               out=waA_scr[sl, c0:c0 + 2].rearrange("c p w -> p c w"),
                               in_=wev[ep][:].rearrange("p (c w) -> p c w", c=2)))
            if sl >= 2:
                sb = sl - 2
                psB4 = [psE, psF, psG, psH]
                for pb in range(8):
                    bp = pb % 2
                    bdep = (f"k_w2o_{sb}_{HC // 2 - 1}", f"k_w2o_{sb}_{HC // 2 - 2}")
                    if pb >= 2:
                        bdep = bdep + (f"k_wbm_{sb}_{pb - 2}_{7}",)
                    elif sb >= 1:
                        bdep = bdep + (f"k_wbm_{sb - 1}_{6 + pb}_{7}",)
                    pp.dma("sync", f"wbi{bp}", key=f"k_wbi_{sb}_{pb}", after=bdep,
                           fn=lambda e, sb=sb, pb=pb, bp=bp: e.dma_start(
                               out=wBi[bp][:],
                               in_=waA_scr[sb, :, 16 * pb:16 * (pb + 1), :]
                               .rearrange("c p w -> c (p w)")))
                    for f4 in range(4):
                        pq = psB4[f4]
                        mdep = ("k_consts", f"k_wbi_{sb}_{pb}")
                        if pb >= 1:
                            mdep = mdep + (f"k_wbe_{sb}_{pb - 1}_{f4}",)
                        elif sb >= 1:
                            mdep = mdep + (f"k_wbe_{sb - 1}_{7}_{f4}",)
                        pp.mm(lambda e, f4=f4, pq=pq, bp=bp: e.matmul(
                            pq[0:64, :], c_h64[:],
                            wBi[bp][:, 512 * f4:512 * (f4 + 1)],
                            start=True, stop=True),
                            key=None, after=mdep)
                        pp.mm(lambda e, f4=f4, pq=pq, bp=bp: e.matmul(
                            pq[64:128, :], c_h64[:],
                            wBi[bp][:, 512 * (f4 + 4):512 * (f4 + 5)],
                            start=True, stop=True),
                            key=f"k_wbm_{sb}_{pb}_{f4}")
                        pp.act(lambda e, f4=f4, pq=pq, bp=bp: e.activation(
                            wBo[bp][:, 512 * f4:512 * (f4 + 1)], pq[:], AF.Copy),
                            key=f"k_wbe_{sb}_{pb}_{f4}",
                            after=(f"k_wbm_{sb}_{pb}_{f4}",) +
                            ((f"k_wbo_{sb}_{pb - 2}",) if pb >= 2 else
                             ((f"k_wbo_{sb - 1}_{6 + pb}",) if sb >= 1 else ())))
                    pp.alias(f"k_wbm_{sb}_{pb}_{7}", f"k_wbm_{sb}_{pb}_{3}")
                    pp.dma("sync", f"wbo{bp}", key=f"k_wbo_{sb}_{pb}_a",
                           after=tuple(f"k_wbe_{sb}_{pb}_{f4}" for f4 in range(4)),
                           fn=lambda e, sb=sb, pb=pb, bp=bp: e.dma_start(
                               out=w2h_scr[:, 16 * pb:16 * pb + 8, sb, :],
                               in_=wBo[bp][0:64, :].rearrange("c (p w) -> c p w", p=8)))
                    pp.dma("sync", f"wbo{bp}", key=f"k_wbo_{sb}_{pb}",
                           fn=lambda e, sb=sb, pb=pb, bp=bp: e.dma_start(
                               out=w2h_scr[:, 16 * pb + 8:16 * (pb + 1), sb, :],
                               in_=wBo[bp][64:128, :].rearrange("c (p w) -> c p w", p=8)))

        pp.barrier()

        pp.pool(lambda e: e.iota(mt_iota[:], pattern=[[1, S]], base=0, channel_multiplier=0,
                                 allow_small_or_imprecise_dtypes=True), key="k_iota")
        pp.pool(lambda e: e.iota(mt_invf[:], pattern=[[1, 32]], base=0, channel_multiplier=0,
                                 allow_small_or_imprecise_dtypes=True), key="k_iinvf")
        pp.act(lambda e: e.activation(mt_invf[:], mt_invf[:], AF.Exp, scale=INVF_SCALE),
               key="k_invf", after=("k_iinvf",))

        def gen_trig(tag, n, cosT, sinT, rowdep, extra=()):
            nb = n // 512
            for b2 in range(nb):
                pp.mm(lambda e, b2=b2: e.matmul(
                    psA[0:32, :], mt_invf[:], mt_iota[0:1, 512 * b2:512 * (b2 + 1)],
                    start=True, stop=True), key=f"k_th_{tag}_{b2}",
                    after=("k_invf", rowdep) + tuple(extra) +
                    ((f"k_the_{tag}_{b2 - 1}",) if b2 >= 1 else ()))
                pp.act(lambda e, b2=b2: e.activation(
                    tth[0:32, 512 * b2:512 * (b2 + 1)], psA[0:32, :], AF.Copy),
                    key=f"k_the_{tag}_{b2}", after=(f"k_th_{tag}_{b2}",))
            kth = f"k_the_{tag}_{nb - 1}"
            pp.dve(lambda e: e.tensor_scalar(ttk[0:32, 0:n], tth[0:32, 0:n],
                                             float(1.0 / TWO_PI), None, ALU.mult),
                   key=f"k_k1_{tag}", after=(kth,))
            pp.dve(lambda e: e.tensor_scalar(ttk[0:32, 0:n], ttk[0:32, 0:n],
                                             MAGIC, MAGIC, ALU.add, ALU.subtract),
                   key=f"k_k2_{tag}", after=(f"k_k1_{tag}",))
            prev = f"k_k2_{tag}"
            for ci, cwc in enumerate((CW1, CW2, CW3)):
                pp.dve(lambda e, cwc=cwc: e.scalar_tensor_tensor(
                    tth[0:32, 0:n], ttk[0:32, 0:n], -cwc, tth[0:32, 0:n],
                    ALU.mult, ALU.add), key=f"k_cw{ci}_{tag}", after=(prev,))
                prev = f"k_cw{ci}_{tag}"
            pp.dve(lambda e: e.tensor_scalar(tth[0:32, 0:n], tth[0:32, 0:n],
                                             float(np.pi), float(-np.pi), ALU.min, ALU.max),
                   key=f"k_cl1_{tag}", after=(prev,))
            prev = f"k_cl1_{tag}"
            pp.act(lambda e: e.activation(sinT[:, 0:n], tth[0:32, 0:n], AF.Sin),
                   key=f"k_sin_{tag}", after=(prev,))
            pp.dve(lambda e: e.tensor_scalar(ttk[0:32, 0:n], tth[0:32, 0:n],
                                             float(np.pi / 2), None, ALU.add),
                   key=f"k_cs1_{tag}", after=(prev,))
            pp.dve(lambda e: e.tensor_scalar(tth[0:32, 0:n], ttk[0:32, 0:n],
                                             float(np.pi), None, ALU.is_gt),
                   key=f"k_cs2_{tag}", after=(f"k_cs1_{tag}", f"k_sin_{tag}"))
            pp.dve(lambda e: e.scalar_tensor_tensor(
                ttk[0:32, 0:n], tth[0:32, 0:n], -TWO_PI, ttk[0:32, 0:n],
                ALU.mult, ALU.add), key=f"k_cs3_{tag}", after=(f"k_cs2_{tag}",))
            pp.dve(lambda e: e.tensor_scalar(ttk[0:32, 0:n], ttk[0:32, 0:n],
                                             float(np.pi), float(-np.pi), ALU.min, ALU.max),
                   key=f"k_cl2_{tag}", after=(f"k_cs3_{tag}",))
            pp.act(lambda e: e.activation(cosT[:, 0:n], ttk[0:32, 0:n], AF.Sin),
                   key=f"k_cos_{tag}", after=(f"k_cl2_{tag}",))

        gen_trig("m", S, mcos, msin, "k_iota")
        pp.dma("sync", "pos", key="k_pos", after=(f"k_th_m_{S // 512 - 1}",),
               fn=lambda e: e.dma_start(out=mt_iota[0:1, 0:OWN], in_=posin))
        gen_trig("o", OWN, ocos, osin, "k_pos", extra=("k_cos_m",))
        pp.barrier()

        pp.dma("pool", "wc", key="k_wc", fn=lambda e: e.dma_start(
            out=wc_sb[:].rearrange("p (c f) -> p c f", c=DC),
            in_=wqkv[:, NH * DQK:].rearrange("(c p) f -> p c f", p=128)))
        pp.dma("pool", "wk", key="k_wk", fn=lambda e: e.dma_start(
            out=wk_sb[:].rearrange("p (c f) -> p c f", c=CC),
            in_=wkup.rearrange("(c p) f -> p c f", p=128)))
        pp.dma("pool", "wv", key="k_wv", fn=lambda e: e.dma_start(
            out=wv_sb[:].rearrange("p (c f) -> p c f", c=CC),
            in_=wvup.rearrange("(c p) f -> p c f", p=128)))
        for fc in range(CC):
            for dcx in range(DC):
                pp.mm(lambda e, fc=fc, dcx=dcx: e.matmul(
                    psB[:, 0:1],
                    wc_sb[:, 512 * dcx + 128 * fc:512 * dcx + 128 * fc + 128],
                    c_ones[:], start=(dcx == 0), stop=(dcx == DC - 1)),
                    key=f"k_wsm_{fc}" if dcx == DC - 1 else None,
                    after=("k_wc", "k_consts",
                           f"k_wsum_{fc - 1}" if fc >= 1 else None) if dcx == 0 else ())
            pp.act(lambda e, fc=fc: e.activation(wsum[:, fc:fc + 1], psB[:, 0:1], AF.Copy),
                   key=f"k_wsum_{fc}", after=(f"k_wsm_{fc}",))
        pp.pool(lambda e: e.memset(kmax[:], 0.0), key="k_km0")

        def load_xs(sp):
            pp.dma("pool", f"xs{sp % 2}", key=f"k_xs_{sp}",
                   after=(f"k_cr_{sp - 2}_{CC - 1}",) if sp >= 2 else (),
                   fn=lambda e, sp=sp: e.dma_start(
                       out=xs_sb[sp % 2][:].rearrange("p (c t) -> p c t", c=DC),
                       in_=xTb[:, 512 * sp:512 * (sp + 1)]
                       .rearrange("(c p) t -> p c t", p=128)))

        def emit_stats(sp):
            xs2 = xs_sb[sp % 2]
            for dcx in range(DC):
                dep0 = (f"k_xs_{sp}", "k_consts")
                if sp >= 1:
                    dep0 = dep0 + (f"k_rowA_{sp - 1}",)
                pp.mm(lambda e, dcx=dcx, xs2=xs2: e.matmul(
                    psG[0:1, :], c_ones[:], xs2[:, 512 * dcx:512 * (dcx + 1)],
                    start=(dcx == 0), stop=(dcx == DC - 1)),
                    key=f"k_sx_{sp}" if dcx == DC - 1 else None,
                    after=dep0 if dcx == 0 else ())
            for dcx in range(DC):
                sl8 = dcx % 8
                sqdep = (f"k_xs_{sp}",)
                if dcx >= 8:
                    sqdep = sqdep + (f"k_sqm_{sp}_{dcx - 8}",)
                elif dcx == 0 and sp >= 1:
                    sqdep = sqdep + (f"k_sqm_{sp - 1}_{DC - 1}",)
                pp.dve(lambda e, sl8=sl8, dcx=dcx, xs2=xs2: e.tensor_mul(
                    sqx[:, 512 * sl8:512 * (sl8 + 1)],
                    xs2[:, 512 * dcx:512 * (dcx + 1)],
                    xs2[:, 512 * dcx:512 * (dcx + 1)]),
                    key=f"k_sq_{sp}_{dcx}", after=sqdep)
                mdep = (f"k_sq_{sp}_{dcx}",)
                if dcx == 0 and sp >= 1:
                    mdep = mdep + (f"k_rowB_{sp - 1}",)
                pp.mm(lambda e, sl8=sl8, dcx=dcx: e.matmul(
                    psH[0:1, :], c_ones[:], sqx[:, 512 * sl8:512 * (sl8 + 1)],
                    start=(dcx == 0), stop=(dcx == DC - 1)),
                    key=f"k_sqm_{sp}_{dcx}", after=mdep)

        load_xs(0)
        load_xs(1)
        emit_stats(0)

        for s_ in range(cfg.NSLAB):
            par = s_ % 2
            xs = xs_sb[par]
            pp.act(lambda e: e.activation(rowsA[:], psG[0:1, :], AF.Copy, scale=1.0 / D),
                   key=f"k_rowA_{s_}", after=(f"k_sx_{s_}",) +
                   ((f"k_ms_{s_ - 1}",) if s_ >= 1 else ()))
            pp.act(lambda e: e.activation(rowsB[:], psH[0:1, :], AF.Copy, scale=1.0 / D),
                   key=f"k_rowB_{s_}", after=(f"k_sqm_{s_}_{DC - 1}",) +
                   ((f"k_ms_{s_ - 1}",) if s_ >= 1 else ()))
            if s_ + 1 < cfg.NSLAB:
                emit_stats(s_ + 1)
            pp.act(lambda e: e.activation(rowsC[:], rowsA[:], AF.Square),
                   key=f"k_rowC_{s_}", after=(f"k_rowA_{s_}",))
            pp.dve(lambda e: e.tensor_sub(rowsB[:], rowsB[:], rowsC[:]),
                   key=f"k_var_{s_}", after=(f"k_rowB_{s_}", f"k_rowC_{s_}"))
            pp.dve(lambda e: e.tensor_scalar(rowsB[:], rowsB[:], EPS, None, ALU.add),
                   key=f"k_vare_{s_}", after=(f"k_var_{s_}",))
            pp.act(lambda e: e.activation(rowsB[:], rowsB[:], AF.Ln),
                   key=f"k_lnv_{s_}", after=(f"k_vare_{s_}",))
            pp.act(lambda e: e.activation(rowsB[:], rowsB[:], AF.Exp, scale=-0.5),
                   key=f"k_istd_{s_}", after=(f"k_lnv_{s_}",))
            pp.dve(lambda e: e.tensor_mul(rowsC[:], rowsA[:], rowsB[:]),
                   key=f"k_ms_{s_}", after=(f"k_istd_{s_}", f"k_rowC_{s_}"))
            depE = (f"k_istd_{s_}", "k_consts")
            if s_ >= 1:
                depE = depE + (f"k_vev_{s_ - 1}_2",)
            pp.mm(lambda e: e.matmul(psE[:], c_onesrow[:], rowsB[:], start=True, stop=True),
                  key=f"k_bs_{s_}", after=depE)
            pp.act(lambda e: e.activation(sS[:], psE[:], AF.Copy),
                   key=f"k_bse_{s_}", after=(f"k_bs_{s_}",))
            pp.mm(lambda e: e.matmul(psE[:], c_onesrow[:], rowsC[:], start=True, stop=True),
                  key=f"k_bm_{s_}", after=(f"k_ms_{s_}", f"k_bse_{s_}"))
            pp.act(lambda e: e.activation(sM[:], psE[:], AF.Copy),
                   key=f"k_bme_{s_}", after=(f"k_bm_{s_}",))
            for fc in range(CC):
                pcs = psA if fc % 2 == 0 else psB
                for dcx in range(DC):
                    dep0 = [f"k_xs_{s_}", "k_wc"]
                    if fc <= 1 and s_ == 0:
                        dep0.append(f"k_wsum_{CC - 1}")
                    if s_ >= 1 or fc >= 2:
                        dep0.append(f"k_cf1_{s_}_{fc - 2}" if fc >= 2 else f"k_cf1_{s_ - 1}_{fc + 2}")
                    pp.mm(lambda e, fc=fc, dcx=dcx, xs=xs, pcs=pcs: e.matmul(
                        pcs[:], wc_sb[:, 512 * dcx + 128 * fc:512 * dcx + 128 * fc + 128],
                        xs[:, 512 * dcx:512 * (dcx + 1)],
                        start=(dcx == 0), stop=(dcx == DC - 1)),
                        key=f"k_cr_{s_}_{fc}" if dcx == DC - 1 else None,
                        after=tuple(dep0) if dcx == 0 else ())
                pp.dve(lambda e, pcs=pcs: e.tensor_mul(xnf[:], pcs[:], sS[:]),
                       key=f"k_cf1_{s_}_{fc}", after=(f"k_cr_{s_}_{fc}", f"k_bse_{s_}"))
                pp.dve(lambda e, fc=fc: e.tensor_scalar(
                    rt1[:], sM[:], wsum[:, fc:fc + 1], None, ALU.mult),
                    key=f"k_cf2_{s_}_{fc}",
                    after=(f"k_bme_{s_}", f"k_wsum_{fc}") +
                    ((f"k_cfix_{s_}_{fc - 1}",) if fc >= 1 else
                     ((f"k_cfix_{s_ - 1}_{CC - 1}",) if s_ >= 1 else ())))
                pp.dve(lambda e, fc=fc: e.tensor_sub(
                    csb[:, 512 * fc:512 * (fc + 1)], xnf[:], rt1[:]),
                    key=f"k_cfix_{s_}_{fc}",
                    after=(f"k_cf1_{s_}_{fc}", f"k_cf2_{s_}_{fc}"))
            cdone = f"k_cfix_{s_}_{CC - 1}"
            if s_ + 2 < cfg.NSLAB:
                load_xs(s_ + 2)
            for g in range(NKV):
                for ccx in range(CC):
                    dep0 = (cdone, "k_wk") if ccx == 0 else ()
                    if ccx == 0 and g >= 1:
                        dep0 = dep0 + (f"k_knsq_{s_}_{g - 1}",)
                    if ccx == 0 and g == 0 and s_ >= 1:
                        dep0 = dep0 + (f"k_knsq_{s_ - 1}_{NKV - 1}",)
                    pp.mm(lambda e, g=g, ccx=ccx: e.matmul(
                        psC[:],
                        wk_sb[:, (NKV * DQK) * ccx + 128 * g:(NKV * DQK) * ccx + 128 * g + 128],
                        csb[:, 512 * ccx:512 * (ccx + 1)],
                        start=(ccx == 0), stop=(ccx == CC - 1)),
                        key=f"k_kn_{s_}_{g}" if ccx == CC - 1 else None,
                        after=dep0)
                pp.act(lambda e: e.activation(kev[:], psC[:], AF.Copy),
                       key=f"k_knev_{s_}_{g}", after=(f"k_kn_{s_}_{g}",) +
                       ((f"k_knw_{s_}_{g - 1}_3",) if g >= 1 else
                        ((f"k_vw_{s_ - 1}_3_{NKV - 1}",) if s_ >= 1 else ())))
                pp.act(lambda e: e.activation(sqs[:], psC[:], AF.Square),
                       key=f"k_knsq_{s_}_{g}", after=(f"k_knev_{s_}_{g}",))
                dep_kn2 = (f"k_knsq_{s_}_{g}",)
                if g == 0:
                    if s_ >= 1:
                        dep_kn2 = dep_kn2 + (f"k_vev_{s_ - 1}_{3}", f"k_kmax_{s_ - 1}")
                pp.mm(lambda e, g=g: e.matmul(
                    psF[0:4, :], c_ind4[:, 4 * g:4 * g + 4], sqs[:],
                    start=(g == 0), stop=False), key=f"k_knm_{s_}_{g}", after=dep_kn2)
                pp.dma("sync", "knw", key=f"k_knw_{s_}_{g}_3",
                       after=(f"k_knev_{s_}_{g}",),
                       fn=lambda e, s_=s_, g=g: e.dma_start(
                           out=kn_scr[4 * s_:4 * s_ + 4, g].rearrange("c p w -> p c w"),
                           in_=kev[:].rearrange("p (c w) -> p c w", c=4)))
            for cx in range(4):
                pp.dma("sync", "trg", key=f"k_cosD_{s_}_{cx}",
                       after=("k_cos_m",) + ((f"k_rope_kr_{s_ - 1}_1",) if s_ >= 1 and cx == 0 else ()),
                       fn=lambda e, s_=s_, cx=cx: e.dma_start(
                           out=cosD[32 * cx:32 * (cx + 1), :],
                           in_=mcos[:, 512 * s_:512 * (s_ + 1)]))
                pp.dma("sync", "trg", key=f"k_sinD_{s_}_{cx}",
                       after=("k_sin_m",) + ((f"k_rope_kr_{s_ - 1}_1",) if s_ >= 1 and cx == 0 else ()),
                       fn=lambda e, s_=s_, cx=cx: e.dma_start(
                           out=sinD[32 * cx:32 * (cx + 1), :],
                           in_=msin[:, 512 * s_:512 * (s_ + 1)]))
            pp.act(lambda e: e.activation(sinD[0:32, :], sinD[0:32, :], AF.Copy, scale=-1.0),
                   key=f"k_sn1_{s_}", after=(f"k_sinD_{s_}_3", f"k_cosD_{s_}_3"))
            pp.act(lambda e: e.activation(sinD[64:96, :], sinD[64:96, :], AF.Copy, scale=-1.0),
                   key=f"k_sn2_{s_}", after=(f"k_sn1_{s_}",))
            for pr in range(NKV // 2):
                for ccx in range(CC):
                    dep0 = (cdone,) if ccx == 0 else ()
                    if ccx == 0 and pr >= 1:
                        dep0 = dep0 + (f"k_krsq_{s_}_{pr - 1}",)
                    if ccx == 0 and pr == 0 and s_ >= 1:
                        dep0 = dep0 + (f"k_krsq_{s_ - 1}_{NKV // 2 - 1}",)
                    pp.mm(lambda e, pr=pr, ccx=ccx: e.matmul(
                        psD[:],
                        wk_sb[:, (NKV * DQK) * ccx + NKV * DN + 128 * pr:
                              (NKV * DQK) * ccx + NKV * DN + 128 * pr + 128],
                        csb[:, 512 * ccx:512 * (ccx + 1)],
                        start=(ccx == 0), stop=(ccx == CC - 1)),
                        key=f"k_kr_{s_}_{pr}" if ccx == CC - 1 else None,
                        after=dep0)
                pp.act(lambda e, pr=pr: e.activation(rraw2[pr % 2][:], psD[:], AF.Copy),
                       key=f"k_krev_{s_}_{pr}", after=(f"k_kr_{s_}_{pr}",) +
                       ((f"k_rope_kr_{s_ - 1}_{pr}", f"k_swp_kr_{s_ - 1}_{pr}_3")
                        if s_ >= 1 else ()))
                pp.act(lambda e: e.activation(sqs[:], psD[:], AF.Square),
                       key=f"k_krsq_{s_}_{pr}", after=(f"k_krev_{s_}_{pr}",))
                pp.mm(lambda e, pr=pr: e.matmul(
                    psF[0:4, :], c_i4r[:, 4 * pr:4 * (pr + 1)], sqs[:],
                    start=False, stop=(pr == NKV // 2 - 1)),
                    key=f"k_krm_{s_}_{pr}", after=(f"k_krsq_{s_}_{pr}",))
                rwdep = ()
                if s_ >= 1:
                    rwdep = (f"k_krw_{s_ - 1}_{pr}_3", f"k_krw2_{s_ - 1}_{pr}_3")
                emit_rope(pp, f"kr_{s_}_{pr}",
                          deps=(f"k_krev_{s_}_{pr}", f"k_sn2_{s_}") + rwdep,
                          out_even=rout2[pr % 2][0:64, :],
                          out_odd=rout2[pr % 2][64:128, :],
                          use_rout=True, pb=pr % 2)
                pp.dma("sync", f"krw{pr}", key=f"k_krw_{s_}_{pr}_3",
                       after=(f"k_rope_kr_{s_}_{pr}",),
                       fn=lambda e, s_=s_, pr=pr: e.dma_start(
                           out=kr_scr[4 * s_:4 * s_ + 4, 2 * pr, 0:64, :]
                           .rearrange("c p w -> p c w"),
                           in_=rout2[pr % 2][0:64, :].rearrange("p (c w) -> p c w", c=4)))
                pp.dma("sync", f"krw{pr}", key=f"k_krw2_{s_}_{pr}_3",
                       after=(f"k_rope_kr_{s_}_{pr}",),
                       fn=lambda e, s_=s_, pr=pr: e.dma_start(
                           out=kr_scr[4 * s_:4 * s_ + 4, 2 * pr + 1, 0:64, :]
                           .rearrange("c p w -> p c w"),
                           in_=rout2[pr % 2][64:128, :].rearrange("p (c w) -> p c w", c=4)))
            pp.dve(lambda e: e.reduce_max(rt2[0:4, 0:1], psF[0:4, :], AXX),
                   key=f"k_kred_{s_}", after=(f"k_krm_{s_}_{NKV // 2 - 1}",))
            pp.dve(lambda e: e.tensor_max(kmax[:], kmax[:], rt2[0:4, 0:1]),
                   key=f"k_kmax_{s_}", after=(f"k_kred_{s_}", "k_km0"))
            for tc in range(4):
                pv = psE if tc % 2 == 0 else psF
                for ccx in range(CC):
                    dep0 = [cdone, "k_wv"]
                    if tc == 0:
                        dep0.append(f"k_bme_{s_}")
                    if tc == 1:
                        dep0.append(f"k_kred_{s_}")
                    if tc >= 2:
                        dep0.append(f"k_vev_{s_}_{tc - 2}")
                    elif s_ >= 1:
                        dep0.append(f"k_vev_{s_ - 1}_{tc + 2}")
                    pp.mm(lambda e, tc=tc, ccx=ccx, pv=pv: e.matmul(
                        pv[:], csb[:, 512 * ccx + 128 * tc:512 * ccx + 128 * tc + 128],
                        wv_sb[:, (NKV * DV) * ccx:(NKV * DV) * (ccx + 1)],
                        start=(ccx == 0), stop=(ccx == CC - 1)),
                        key=f"k_v_{s_}_{tc}" if ccx == CC - 1 else None,
                        after=tuple(dep0) if ccx == 0 else ())
                pp.act(lambda e, pv=pv: e.activation(kev[:], pv[:], AF.Copy),
                       key=f"k_vev_{s_}_{tc}", after=(f"k_v_{s_}_{tc}",) +
                       ((f"k_vw_{s_}_{tc - 1}_{NKV - 1}",) if tc >= 1 else
                        ((f"k_knw_{s_}_{NKV - 1}_3",) if True else ())))
                pp.dma("sync", "vw", key=f"k_vw_{s_}_{tc}_{NKV - 1}",
                       after=(f"k_vev_{s_}_{tc}",),
                       fn=lambda e, s_=s_, tc=tc: e.dma_start(
                           out=v_scr[4 * s_ + tc].rearrange("g p w -> p g w"),
                           in_=kev[:].rearrange("p (g w) -> p g w", g=4)))
            pp.alias(f"k_vq_{s_}", f"k_v_{s_}_3")
        pp.dve(lambda e: e.tensor_scalar(kmax[:], kmax[:], 1e-30, None, ALU.add),
               key="k_kma", after=(f"k_kmax_{cfg.NSLAB - 1}",))
        pp.act(lambda e: e.activation(kmax[:], kmax[:], AF.Ln),
               key="k_kml", after=("k_kma",))
        pp.act(lambda e: e.activation(kmax[:], kmax[:], AF.Exp, scale=0.5),
               key="k_kme", after=("k_kml",))
        pp.mm(lambda e: e.matmul(psG[0:16, 0:1], c_exp4[:], kmax[:], start=True, stop=True),
              key="k_kmx", after=("k_kme", f"k_kred_{cfg.NSLAB - 1}"))
        pp.act(lambda e: e.activation(nkmax[:], psG[0:16, 0:1], AF.Copy, scale=-1.0),
               key="k_nkm", after=("k_kmx",))
        pp.barrier()

        def load_xo(ep2):
            pp.dma("pool", f"xs{ep2 % 2}", key=f"k_xo_{ep2}",
                   after=(f"k_qxn_{ep2 - 2}_{DC - 1}",) if ep2 >= 2 else (),
                   fn=lambda e, ep2=ep2: e.dma_start(
                       out=xs_sb[ep2 % 2][:].rearrange("p (c t) -> p c t", c=DC),
                       in_=xTo[:, 512 * ep2:512 * (ep2 + 1)]
                       .rearrange("(c p) t -> p c t", p=128)))

        load_xo(0)
        load_xo(1)
        for e_ in range(4):
            par = e_ % 2
            xs = xs_sb[par]
            for dcx in range(DC):
                dep0 = (f"k_xo_{e_}",)
                if e_ >= 1:
                    dep0 = dep0 + (f"k_orowA_{e_ - 1}",)
                pp.mm(lambda e, dcx=dcx, xs=xs: e.matmul(
                    psG[0:1, :], c_ones[:], xs[:, 512 * dcx:512 * (dcx + 1)],
                    start=(dcx == 0), stop=(dcx == DC - 1)),
                    key=f"k_osx_{e_}" if dcx == DC - 1 else None,
                    after=dep0 if dcx == 0 else ())
                pp.act(lambda e, dcx=dcx, xs=xs: e.activation(
                    sqs[:], xs[:, 512 * dcx:512 * (dcx + 1)], AF.Square),
                    key=f"k_osq_{e_}_{dcx}",
                    after=(f"k_xo_{e_}",) if dcx == 0 else (f"k_osqm_{e_}_{dcx - 1}",))
                pp.mm(lambda e: e.matmul(psH[0:1, :], c_ones[:], sqs[:],
                                         start=(dcx == 0), stop=(dcx == DC - 1)),
                      key=f"k_osqm_{e_}_{dcx}",
                      after=(f"k_osq_{e_}_{dcx}",) +
                      ((f"k_orowB_{e_ - 1}",) if dcx == 0 and e_ >= 1 else ()))
            pp.act(lambda e: e.activation(rowsA[:], psG[0:1, :], AF.Copy, scale=1.0 / D),
                   key=f"k_orowA_{e_}", after=(f"k_osx_{e_}",) +
                   ((f"k_oms_{e_ - 1}",) if e_ >= 1 else ()))
            pp.act(lambda e: e.activation(rowsB[:], psH[0:1, :], AF.Copy, scale=1.0 / D),
                   key=f"k_orowB_{e_}", after=(f"k_osqm_{e_}_{DC - 1}",) +
                   ((f"k_oms_{e_ - 1}",) if e_ >= 1 else ()))
            pp.act(lambda e: e.activation(rowsC[:], rowsA[:], AF.Square),
                   key=f"k_orowC_{e_}", after=(f"k_orowA_{e_}",))
            pp.dve(lambda e: e.tensor_sub(rowsB[:], rowsB[:], rowsC[:]),
                   key=f"k_ovar_{e_}", after=(f"k_orowB_{e_}", f"k_orowC_{e_}"))
            pp.dve(lambda e: e.tensor_scalar(rowsB[:], rowsB[:], EPS, None, ALU.add),
                   key=f"k_ovare_{e_}", after=(f"k_ovar_{e_}",))
            pp.act(lambda e: e.activation(rowsB[:], rowsB[:], AF.Ln),
                   key=f"k_olnv_{e_}", after=(f"k_ovare_{e_}",))
            pp.act(lambda e: e.activation(rowsB[:], rowsB[:], AF.Exp, scale=-0.5),
                   key=f"k_oistd_{e_}", after=(f"k_olnv_{e_}",))
            pp.dve(lambda e: e.tensor_mul(rowsC[:], rowsA[:], rowsB[:]),
                   key=f"k_oms_{e_}", after=(f"k_oistd_{e_}", f"k_orowC_{e_}"))
            depE = (f"k_oistd_{e_}",)
            if e_ >= 1:
                depE = depE + (f"k_obme_{e_ - 1}",)
            pp.mm(lambda e: e.matmul(psE[:], c_onesrow[:], rowsB[:], start=True, stop=True),
                  key=f"k_obs_{e_}", after=depE)
            pp.act(lambda e: e.activation(sS[:], psE[:], AF.Copy),
                   key=f"k_obse_{e_}", after=(f"k_obs_{e_}",))
            pp.mm(lambda e: e.matmul(psE[:], c_onesrow[:], rowsC[:], start=True, stop=True),
                  key=f"k_obm_{e_}", after=(f"k_oms_{e_}", f"k_obse_{e_}"))
            pp.act(lambda e: e.activation(sM[:], psE[:], AF.Copy),
                   key=f"k_obme_{e_}", after=(f"k_obm_{e_}",))
            for dcx in range(DC):
                pp.dve(lambda e, dcx=dcx, xs=xs: e.tensor_mul(
                    xnf[:], xs[:, 512 * dcx:512 * (dcx + 1)], sS[:]),
                    key=f"k_qxm_{e_}_{dcx}",
                    after=(f"k_obse_{e_}",) if dcx == 0 else (f"k_qxn_{e_}_{dcx - 1}",))
                pp.dve(lambda e, dcx=dcx: e.tensor_sub(
                    xno[:, 512 * dcx:512 * (dcx + 1)], xnf[:], sM[:]),
                    key=f"k_qxn_{e_}_{dcx}",
                    after=(f"k_qxm_{e_}_{dcx}", f"k_obme_{e_}") +
                    ((f"k_qrmm_{e_ - 1}_{NH // 2 - 1}",) if dcx == 0 and e_ >= 1 else ()))
            xndone = f"k_qxn_{e_}_{DC - 1}"
            for cx in range(4):
                pp.dma("sync", "trg", key=f"k_ocosD_{e_}_{cx}",
                       after=("k_cos_o",) + ((f"k_rope_qr_{e_ - 1}_{NH // 2 - 1}",)
                                             if e_ >= 1 and cx == 0 else ()),
                       fn=lambda e, e_=e_, cx=cx: e.dma_start(
                           out=cosD[32 * cx:32 * (cx + 1), :],
                           in_=ocos[:, 512 * e_:512 * (e_ + 1)]))
                pp.dma("sync", "trg", key=f"k_osinD_{e_}_{cx}",
                       after=("k_sin_o",) + ((f"k_rope_qr_{e_ - 1}_{NH // 2 - 1}",)
                                             if e_ >= 1 and cx == 0 else ()),
                       fn=lambda e, e_=e_, cx=cx: e.dma_start(
                           out=sinD[32 * cx:32 * (cx + 1), :],
                           in_=osin[:, 512 * e_:512 * (e_ + 1)]))
            pp.act(lambda e: e.activation(sinD[0:32, :], sinD[0:32, :], AF.Copy, scale=-1.0),
                   key=f"k_osn1_{e_}", after=(f"k_osinD_{e_}_3", f"k_ocosD_{e_}_3"))
            pp.act(lambda e: e.activation(sinD[64:96, :], sinD[64:96, :], AF.Copy, scale=-1.0),
                   key=f"k_osn2_{e_}", after=(f"k_osn1_{e_}",))
            for h in range(NH):
                wpar = h % 2
                pq = psA if h % 2 == 0 else psB
                wdep = (f"k_qnmm_{e_}_{h - 2}",) if h >= 2 else \
                    ((f"k_qrmm_{e_ - 1}_{NH // 2 - 2 + h}",) if e_ >= 1 else ())
                pp.dma("pool", f"wq{wpar}", key=f"k_wq_{e_}_{h}", after=wdep,
                       fn=lambda e, h=h, wpar=wpar: e.dma_start(
                           out=wqs[wpar][:].rearrange("p (c f) -> p c f", c=DC),
                           in_=wqkv[:, 128 * h:128 * (h + 1)].rearrange("(c p) f -> p c f", p=128)))
                for dcx in range(DC):
                    dep0 = (f"k_wq_{e_}_{h}", f"k_qxn_{e_}_{dcx}")
                    if h >= 2 and dcx == 0:
                        dep0 = dep0 + (f"k_qsq_{e_}_{h - 2}",)
                    pp.mm(lambda e, dcx=dcx, wpar=wpar, pq=pq: e.matmul(
                        pq[:], wqs[wpar][:, 128 * dcx:128 * (dcx + 1)],
                        xno[:, 512 * dcx:512 * (dcx + 1)],
                        start=(dcx == 0), stop=(dcx == DC - 1)),
                        key=f"k_qnmm_{e_}_{h}" if dcx == DC - 1 else None,
                        after=dep0)
                pp.act(lambda e, pq=pq: e.activation(qev[:], pq[:], AF.Copy, scale=SCALE),
                       key=f"k_qev_{e_}_{h}", after=(f"k_qnmm_{e_}_{h}",) +
                       ((f"k_qnw_{e_}_{h - 1}",) if h >= 1 else
                        ((f"k_qnw_{e_ - 1}_{NH - 1}",) if e_ >= 1 else ())))
                pp.act(lambda e, pq=pq: e.activation(sqs[:], pq[:], AF.Square, scale=SCALE),
                       key=f"k_qsq_{e_}_{h}", after=(f"k_qev_{e_}_{h}",))
                dep_q16 = (f"k_qsq_{e_}_{h}",)
                if h == 0:
                    dep_q16 = dep_q16 + ("k_nkm",)
                    if e_ >= 1:
                        dep_q16 = dep_q16 + (f"k_mhat_{e_ - 1}",)
                pp.mm(lambda e, h=h: e.matmul(
                    psF[0:16, :], c_i16n[:, 16 * h:16 * (h + 1)], sqs[:],
                    start=(h == 0), stop=False), key=f"k_qn16_{e_}_{h}", after=dep_q16)
                pp.dma("pool", "qnw", key=f"k_qnw_{e_}_{h}", after=(f"k_qev_{e_}_{h}",),
                       fn=lambda e, e_=e_, h=h: e.dma_start(out=qn_scr[e_, h], in_=qev[:]))
            if e_ + 2 < 4:
                load_xo(e_ + 2)
            for pr in range(NH // 2):
                wpar = pr % 2
                pq = psA if pr % 2 == 0 else psB
                pp.dma("pool", f"wq{wpar}", key=f"k_wqr_{e_}_{pr}",
                       after=(f"k_qrmm_{e_}_{pr - 2}",) if pr >= 2 else
                       (f"k_qnmm_{e_}_{NH - 2 + pr}",),
                       fn=lambda e, pr=pr, wpar=wpar: e.dma_start(
                           out=wqs[wpar][:].rearrange("p (c f) -> p c f", c=DC),
                           in_=wqkv[:, NH * DN + 128 * pr:NH * DN + 128 * (pr + 1)]
                           .rearrange("(c p) f -> p c f", p=128)))
                for dcx in range(DC):
                    dep0 = (f"k_wqr_{e_}_{pr}",)
                    if pr >= 2:
                        dep0 = dep0 + (f"k_qrsq_{e_}_{pr - 2}",)
                    else:
                        dep0 = dep0 + (f"k_qsq_{e_}_{NH - 2 + pr}",)
                    pp.mm(lambda e, dcx=dcx, wpar=wpar, pq=pq: e.matmul(
                        pq[:], wqs[wpar][:, 128 * dcx:128 * (dcx + 1)],
                        xno[:, 512 * dcx:512 * (dcx + 1)],
                        start=(dcx == 0), stop=(dcx == DC - 1)),
                        key=f"k_qrmm_{e_}_{pr}" if dcx == DC - 1 else None,
                        after=dep0 if dcx == 0 else ())
                pp.act(lambda e, pq=pq, pr=pr: e.activation(
                    rraw2[pr % 2][:], pq[:], AF.Copy, scale=SCALE),
                    key=f"k_qrev_{e_}_{pr}", after=(f"k_qrmm_{e_}_{pr}",) +
                    ((f"k_rope_qr_{e_}_{pr - 2}", f"k_swp_qr_{e_}_{pr - 2}_3") if pr >= 2 else
                     ((f"k_rope_qr_{e_ - 1}_{NH // 2 - 2 + pr}",
                       f"k_swp_qr_{e_ - 1}_{NH // 2 - 2 + pr}_3") if e_ >= 1 else ())))
                pp.act(lambda e, pq=pq: e.activation(sqs[:], pq[:], AF.Square, scale=SCALE),
                       key=f"k_qrsq_{e_}_{pr}", after=(f"k_qrev_{e_}_{pr}",))
                pp.mm(lambda e, pr=pr: e.matmul(
                    psF[0:16, :], c_i16r[:, 16 * pr:16 * (pr + 1)], sqs[:],
                    start=False, stop=(pr == NH // 2 - 1)),
                    key=f"k_qr16_{e_}_{pr}", after=(f"k_qrsq_{e_}_{pr}",))
                emit_rope(pp, f"qr_{e_}_{pr}",
                          deps=(f"k_qrev_{e_}_{pr}", f"k_osn2_{e_}") +
                          ((f"k_qrw_{e_ - 1}",) if e_ >= 1 and pr <= 1 else ()),
                          out_even=qaug[0:64, 512 * (2 * pr):512 * (2 * pr) + 512],
                          out_odd=qaug[0:64, 512 * (2 * pr + 1):512 * (2 * pr + 1) + 512],
                          use_rout=False, pb=pr % 2)
            pp.act(lambda e: e.activation(qn2e[:], psF[0:16, :], AF.Copy),
                   key=f"k_qcp_{e_}", after=(f"k_qr16_{e_}_{NH // 2 - 1}",))
            pp.dve(lambda e: e.tensor_scalar(qn2e[:], qn2e[:], 1e-30, None, ALU.add),
                   key=f"k_qeps_{e_}", after=(f"k_qcp_{e_}",))
            pp.act(lambda e: e.activation(qn2e[:], qn2e[:], AF.Ln),
                   key=f"k_qln_{e_}", after=(f"k_qeps_{e_}",))
            pp.act(lambda e: e.activation(qn2e[:], qn2e[:], AF.Exp, scale=0.5),
                   key=f"k_qnorm_{e_}", after=(f"k_qln_{e_}",))
            pp.dve(lambda e: e.tensor_scalar(qn2e[:], qn2e[:], nkmax[:], None, ALU.mult),
                   key=f"k_mhat_{e_}", after=(f"k_qnorm_{e_}", "k_nkm"))
            for h in range(NH):
                pp.dma("pool", "mh", key=f"k_mh_{e_}_{h}",
                       after=(f"k_mhat_{e_}", f"k_rope_qr_{e_}_{NH // 2 - 1}") +
                       ((f"k_qrw_{e_ - 1}",) if e_ >= 1 and h == 0 else ()),
                       fn=lambda e, h=h: e.dma_start(
                           out=qaug[64:65, 512 * h:512 * (h + 1)],
                           in_=qn2e[h:h + 1, :]))
            pp.dma("pool", "qrw", key=f"k_qrw_{e_}",
                   after=tuple(f"k_mh_{e_}_{h}" for h in range(NH)),
                   fn=lambda e, e_=e_: e.dma_start(
                       out=qr_scr[e_].rearrange("h r t -> r h t"),
                       in_=qaug[:].rearrange("r (h t) -> r h t", h=NH)))
        pp.barrier()

        ps_o = [psC, psD, psE, psF]
        ps_sc = [psA, psB, psG, psA]
        exuse = {}
        scuse = {}
        tailuse = {}
        diaguse = {}
        lastago = None
        for e_ in range(4):
            L = cfg.L[e_]
            NT = L - 4
            dpar = e_ % 2
            Lp = cfg.L[e_ - 1] if e_ >= 1 else 0
            qdep = (f"k_sc_{e_ - 1}_{NKV - 1}_{Lp - 1}_{3}",) if e_ >= 1 else ()
            pp.dma("act", "bq", key=f"k_bqn_{e_}", after=qdep,
                   fn=lambda e, e_=e_: e.dma_start(
                       out=qn_sb[:].rearrange("p (h t) -> p h t", h=NH),
                       in_=qn_scr[e_].rearrange("h p t -> p h t")))
            pp.dma("act", "bq", key=f"k_bqr_{e_}",
                   fn=lambda e, e_=e_: e.dma_start(
                       out=qr_sb[:].rearrange("r (h t) -> r h t", h=NH),
                       in_=qr_scr[e_].rearrange("h r t -> r h t")))
            agdep = ((lastago,) if lastago else ()) + \
                ((f"k_agf_{e_ - 1}",) if e_ >= 1 else ())
            pp.dma("pool", "agf", key=f"k_agf_{e_}", after=agdep,
                   fn=lambda e, e_=e_: e.dma_start(
                       out=aug_dram[:],
                       in_=i_augv[DynSlice(s4_sv[e_], 1), 0:TMAX * 128]))
            ddep = (diaguse[dpar],) if dpar in diaguse else ()
            pp.dma("sync", f"dg{dpar}", key=f"k_dg_{e_}", after=ddep,
                   fn=lambda e, e_=e_, dpar=dpar: e.dma_start(
                       out=b_diag[dpar][:].rearrange("p (cg w) -> p cg w", w=384),
                       in_=kvz_cg[DynSlice(s16_sv[e_], 16)]
                       .rearrange("cg p w -> p cg w")))
            for g in range(NKV):
                tpar = (NKV * e_ + g) % 2
                if NT > 0:
                    tdep = (tailuse[tpar],) if tpar in tailuse else ()
                    pp.dma("sync", f"tl{tpar}", key=f"k_tl_{e_}_{g}", after=tdep,
                           fn=lambda e, g=g, tpar=tpar, NT=NT: e.dma_start(
                               out=b_tail[tpar][:, 0:NT * 384]
                               .rearrange("p (c w) -> p c w", w=384),
                               in_=kvz_scr[0:NT, g].rearrange("c p w -> p c w")))
                    pp.dma("pool", f"ago{tpar}", key=f"k_ago_{e_}_{g}",
                           after=(f"k_tl_{e_}_{g}", f"k_agf_{e_}"),
                           fn=lambda e, tpar=tpar, NT=NT: e.dma_start(
                               out=b_tail[tpar][64:65, 0:NT * 384]
                               .rearrange("one (c w) -> one c w", w=384)[:, :, 128:256],
                               in_=aug_dram[0:1, 0:NT * 128]
                               .rearrange("one (c w) -> one c w", w=128)))
                    lastago = f"k_ago_{e_}_{g}"
                for i in range(L):
                    if i < 4:
                        kvb = b_diag[dpar]
                        base = (4 * i + g) * 384
                        kvdep = f"k_dg_{e_}"
                    else:
                        kvb = b_tail[tpar]
                        base = (i - 4) * 384
                        kvdep = f"k_ago_{e_}_{g}"
                    epar = i % 2
                    for hh in range(4):
                        h = 4 * g + hh
                        sb_i = [0, 1, 2, 0][hh]
                        sc = ps_sc[sb_i]
                        scdep = scuse.get(sb_i)
                        pp.mm(lambda e, h=h, sc=sc, kvb=kvb, base=base: e.matmul(
                            sc[:], kvb[:, base:base + 128],
                            qn_sb[:, 512 * h:512 * (h + 1)],
                            start=True, stop=False), key=None,
                            after=(kvdep, f"k_bqn_{e_}", f"k_bqr_{e_}", scdep))
                        pp.mm(lambda e, h=h, sc=sc, kvb=kvb, base=base: e.matmul(
                            sc[:], kvb[0:65, base + 128:base + 256],
                            qr_sb[:, 512 * h:512 * (h + 1)],
                            start=False, stop=True), key=f"k_sc_{e_}_{g}_{i}_{hh}")
                        exdep = exuse.get((epar, hh))
                        pp.act(lambda e, hh=hh, sc=sc, epar=epar: e.activation(
                            ex_sb[epar][:, 512 * hh:512 * (hh + 1)], sc[:], AF.Exp),
                            key=f"k_exp_{e_}_{g}_{i}_{hh}",
                            after=(f"k_sc_{e_}_{g}_{i}_{hh}", exdep))
                        scuse[sb_i] = f"k_exp_{e_}_{g}_{i}_{hh}"
                    mkey = None
                    if i < 4:
                        mkey = f"k_msk_{e_}_{g}_{i}"
                        mi = i
                        pp.dve(lambda e, epar=epar, mi=mi: e.tensor_tensor(
                            ex_sb[epar][:].rearrange("p (h t) -> p h t", h=4),
                            ex_sb[epar][:].rearrange("p (h t) -> p h t", h=4),
                            c_masks[:, 512 * mi:512 * (mi + 1)]
                            .rearrange("p (o t) -> p o t", o=1).broadcast_to([128, 4, 512]),
                            ALU.mult), key=mkey,
                            after=tuple(f"k_exp_{e_}_{g}_{i}_{hh}" for hh in range(4)))
                    for hh in range(4):
                        dep = (mkey,) if mkey else (f"k_exp_{e_}_{g}_{i}_{hh}",)
                        if i == 0:
                            dep = dep + ((f"k_otf_{e_}_{g - 1}_{hh}",) if g >= 1 else
                                         ((f"k_otf_{e_ - 1}_{NKV - 1}_{hh}",) if e_ >= 1 else ()))
                        pp.mm(lambda e, hh=hh, kvb=kvb, base=base, epar=epar, i=i, L=L: e.matmul(
                            ps_o[hh][:], kvb[:, base + 256:base + 384],
                            ex_sb[epar][:, 512 * hh:512 * (hh + 1)],
                            start=(i == 0), stop=(i == L - 1)),
                            key=f"k_av_{e_}_{g}_{i}_{hh}", after=dep)
                        ddep2 = (mkey,) if mkey else (f"k_exp_{e_}_{g}_{i}_{hh}",)
                        if i == 0 and hh == 0:
                            ddep2 = ddep2 + ((f"k_dn4_{e_}_{g - 1}_{3}",) if g >= 1 else
                                             ((f"k_dn4_{e_ - 1}_{NKV - 1}_{3}",) if e_ >= 1 else ()))
                        if i == 0:
                            pp.dve(lambda e, hh=hh, epar=epar: e.tensor_copy(
                                exa_sb[:, 512 * hh:512 * (hh + 1)],
                                ex_sb[epar][:, 512 * hh:512 * (hh + 1)]),
                                key=f"k_avd_{e_}_{g}_{i}_{hh}", after=ddep2)
                        else:
                            pp.dve(lambda e, hh=hh, epar=epar: e.tensor_add(
                                exa_sb[:, 512 * hh:512 * (hh + 1)],
                                exa_sb[:, 512 * hh:512 * (hh + 1)],
                                ex_sb[epar][:, 512 * hh:512 * (hh + 1)]),
                                key=f"k_avd_{e_}_{g}_{i}_{hh}", after=ddep2)
                        exuse[(epar, hh)] = f"k_avd_{e_}_{g}_{i}_{hh}"
                if NT > 0:
                    tailuse[tpar] = f"k_av_{e_}_{g}_{L - 1}_{3}"
                for hh in range(4):
                    pp.mm(lambda e, hh=hh: e.matmul(
                        psG[0:4, :], c_ind4f[:, 4 * hh:4 * (hh + 1)],
                        exa_sb[:, 512 * hh:512 * (hh + 1)],
                        start=(hh == 0), stop=(hh == 3)),
                        key=f"k_dn4_{e_}_{g}_{hh}",
                        after=(f"k_avd_{e_}_{g}_{L - 1}_{hh}",) +
                        ((f"k_exp_{e_}_{g}_{L - 1}_{2}",) if hh == 0 else ()) +
                        ((f"k_rcp_{e_}_{g - 1}",) if hh == 0 and g >= 1 else
                         ((f"k_rcp_{e_ - 1}_{NKV - 1}",) if hh == 0 and e_ >= 1 else ())))
                pp.dve(lambda e: e.reciprocal(rcp_sb[:], psG[0:4, :]),
                       key=f"k_rcp_{e_}_{g}", after=(f"k_dn4_{e_}_{g}_{3}",))
                scuse[2] = f"k_rcp_{e_}_{g}"
                for hh in range(4):
                    h = 4 * g + hh
                    bdep = (f"k_rcp_{e_}_{g}",)
                    if hh >= 1:
                        bdep = bdep + (f"k_otf_{e_}_{g}_{hh - 1}",)
                    elif g >= 1:
                        bdep = bdep + (f"k_otf_{e_}_{g - 1}_{3}",)
                    elif e_ >= 1:
                        bdep = bdep + (f"k_otf_{e_ - 1}_{NKV - 1}_{3}",)
                    pp.mm(lambda e, hh=hh: e.matmul(
                        psH[:], c_indrow[:, 128 * hh:128 * (hh + 1)], rcp_sb[:],
                        start=True, stop=True), key=f"k_bc_{e_}_{g}_{hh}", after=bdep)
                    op2 = hh % 2
                    odep = ()
                    if hh >= 2:
                        odep = (f"k_otw_{e_}_{g}_{hh - 2}",)
                    elif g >= 1:
                        odep = (f"k_otw_{e_}_{g - 1}_{hh + 2}",)
                    elif e_ >= 1:
                        odep = (f"k_otw_{e_ - 1}_{NKV - 1}_{hh + 2}",)
                    bedep = (f"k_otf_{e_}_{g}_{hh - 1}",) if hh >= 1 else \
                        ((f"k_otf_{e_}_{g - 1}_{3}",) if g >= 1 else
                         ((f"k_otf_{e_ - 1}_{NKV - 1}_{3}",) if e_ >= 1 else ()))
                    pp.act(lambda e: e.activation(bce_sb[:], psH[:], AF.Copy),
                           key=f"k_bce_{e_}_{g}_{hh}",
                           after=(f"k_bc_{e_}_{g}_{hh}",) + bedep)
                    pp.dve(lambda e, hh=hh, op2=op2: e.tensor_mul(
                        otf_sb[op2][:], ps_o[hh][:], bce_sb[:]),
                        key=f"k_otf_{e_}_{g}_{hh}",
                        after=(f"k_bce_{e_}_{g}_{hh}", f"k_av_{e_}_{g}_{L - 1}_{hh}") + odep)
                    pp.dma("pool", f"otw{op2}", key=f"k_otw_{e_}_{g}_{hh}",
                           after=(f"k_otf_{e_}_{g}_{hh}",),
                           fn=lambda e, e_=e_, h=h, op2=op2: e.dma_start(
                               out=ot_scr[h, :, 512 * e_:512 * (e_ + 1)],
                               in_=otf_sb[op2][:]))
            diaguse[dpar] = f"k_av_{e_}_{NKV - 1}_{3}_{3}"
            pp.alias(f"k_bdone_{e_}", f"k_otw_{e_}_{NKV - 1}_{3}")
        pp.barrier()

        pp.dma("pool", "wol", key="k_woL", fn=lambda e: e.dma_start(
            out=wo_sb[:].rearrange("p (c d) -> p c d", c=NH),
            in_=wo.rearrange("(c p) d -> p c d", p=128)))
        pp.dma("sync", "otx", key="k_otxL", fn=lambda e: e.dma_start(
            out=otx_sb[:].rearrange("p (h t) -> p h t", h=NH),
            in_=ot_scr.rearrange("h p t -> p h t")))
        for dcx in range(DC):
            pb = [psA, psB, psC, psD] if dcx % 2 == 0 else [psE, psF, psG, psH]
            xpar = dcx % 2
            pp.dma("act", f"xoc{xpar}", key=f"k_cx_{dcx}",
                   after=(f"k_x2w_{dcx - 2}",) if dcx >= 2 else (),
                   fn=lambda e, dcx=dcx, xpar=xpar: e.dma_start(
                       out=xoc[xpar][:], in_=xTo[128 * dcx:128 * (dcx + 1), :]))
            for hc in range(NH):
                dep0 = ()
                if hc == 0:
                    dep0 = ("k_woL", "k_otxL")
                    if dcx >= 2:
                        dep0 = dep0 + (f"k_x2a_{dcx - 2}_{3}",)
                for cb in range(4):
                    pp.mm(lambda e, dcx=dcx, hc=hc, cb=cb, pb=pb: e.matmul(
                        pb[cb][:],
                        wo_sb[:, D * hc + 128 * dcx:D * hc + 128 * dcx + 128],
                        otx_sb[:, OWN * hc + 512 * cb:OWN * hc + 512 * (cb + 1)],
                        start=(hc == 0), stop=(hc == NH - 1)),
                        key=f"k_wo_{dcx}_{cb}" if hc == NH - 1 else None,
                        after=dep0 if (hc == 0 and cb == 0) else ())
            for cb in range(4):
                pp.dve(lambda e, cb=cb, pb=pb, xpar=xpar: e.tensor_add(
                    x2e[:, 512 * cb:512 * (cb + 1)], pb[cb][:],
                    xoc[xpar][:, 512 * cb:512 * (cb + 1)]),
                    key=f"k_x2a_{dcx}_{cb}",
                    after=(f"k_wo_{dcx}_{cb}", f"k_cx_{dcx}") +
                    ((f"k_x2w_{dcx - 1}",) if cb == 0 and dcx >= 1 else ()))
            pp.dma("pool", "x2w", key=f"k_x2w_{dcx}",
                   after=tuple(f"k_x2a_{dcx}_{cb}" for cb in range(4)),
                   fn=lambda e, dcx=dcx: e.dma_start(out=x2_scr[dcx], in_=x2e[:]))
        pp.barrier()

        for dcx in range(DC):
            xp2 = dcx % 2
            pp.dma("pool", f"x2b{xp2}", key=f"k_x2b_{dcx}",
                   after=(f"k_st1_{dcx - 2}", f"k_st2_{dcx - 2}") if dcx >= 2 else (),
                   fn=lambda e, dcx=dcx, xp2=xp2: e.dma_start(
                       out=x2b[xp2][:], in_=x2_scr[dcx]))
            pp.act(lambda e, xp2=xp2: e.activation(sq2[xp2][:], x2b[xp2][:], AF.Square),
                   key=f"k_sq2_{dcx}", after=(f"k_x2b_{dcx}",) +
                   ((f"k_st2_{dcx - 2}",) if dcx >= 2 else ()))
            for cb in range(4):
                pp.mm(lambda e, cb=cb, xp2=xp2: e.matmul(
                    ps[cb][0:1, :], c_ones[:], x2b[xp2][:, 512 * cb:512 * (cb + 1)],
                    start=(dcx == 0), stop=(dcx == DC - 1)),
                    key=f"k_st1_{dcx}" if cb == 3 else None,
                    after=(f"k_x2b_{dcx}",) if cb == 0 else ())
                pp.mm(lambda e, cb=cb, xp2=xp2: e.matmul(
                    ps[4 + cb][0:1, :], c_ones[:], sq2[xp2][:, 512 * cb:512 * (cb + 1)],
                    start=(dcx == 0), stop=(dcx == DC - 1)),
                    key=f"k_st2_{dcx}" if cb == 3 else None,
                    after=(f"k_sq2_{dcx}",) if cb == 0 else ())
        for cb in range(4):
            pp.act(lambda e, cb=cb: e.activation(
                rows2[0:1, 512 * cb:512 * (cb + 1)], ps[cb][0:1, :], AF.Copy, scale=1.0 / D),
                key=f"k_r2a_{cb}", after=(f"k_st1_{DC - 1}", f"k_st2_{DC - 1}"))
            pp.act(lambda e, cb=cb: e.activation(
                rows2b[0:1, 512 * cb:512 * (cb + 1)], ps[4 + cb][0:1, :], AF.Copy, scale=1.0 / D),
                key=f"k_r2b_{cb}", after=(f"k_st2_{DC - 1}",))
        pp.act(lambda e: e.activation(rows2c[:], rows2[:], AF.Square),
               key="k_r2c", after=tuple(f"k_r2a_{cb}" for cb in range(4)))
        pp.dve(lambda e: e.tensor_sub(rows2b[:], rows2b[:], rows2c[:]),
               key="k_var2", after=("k_r2c",) + tuple(f"k_r2b_{cb}" for cb in range(4)))
        pp.dve(lambda e: e.tensor_scalar(rows2b[:], rows2b[:], EPS, None, ALU.add),
               key="k_var2e", after=("k_var2",))
        pp.act(lambda e: e.activation(rows2b[:], rows2b[:], AF.Ln),
               key="k_lnv2", after=("k_var2e",))
        pp.act(lambda e: e.activation(rows2b[:], rows2b[:], AF.Exp, scale=-0.5),
               key="k_istd2", after=("k_lnv2",))
        pp.dve(lambda e: e.tensor_mul(rows2c[:], rows2[:], rows2b[:]),
               key="k_ms2", after=("k_istd2",))
        for cb in range(4):
            pp.mm(lambda e, cb=cb: e.matmul(
                ps[cb][:], c_onesrow[:], rows2b[0:1, 512 * cb:512 * (cb + 1)],
                start=True, stop=True), key=f"k_b2s_{cb}",
                after=("k_istd2", f"k_r2a_{cb}"))
            pp.act(lambda e, cb=cb: e.activation(
                S2b[:, 512 * cb:512 * (cb + 1)], ps[cb][:], AF.Copy),
                key=f"k_b2se_{cb}", after=(f"k_b2s_{cb}",))
            pp.mm(lambda e, cb=cb: e.matmul(
                ps[4 + cb][:], c_onesrow[:], rows2c[0:1, 512 * cb:512 * (cb + 1)],
                start=True, stop=True), key=f"k_b2m_{cb}",
                after=("k_ms2", f"k_r2b_{cb}"))
            pp.act(lambda e, cb=cb: e.activation(
                M2b[:, 512 * cb:512 * (cb + 1)], ps[4 + cb][:], AF.Copy),
                key=f"k_b2me_{cb}", after=(f"k_b2m_{cb}",))
        for dcx in range(DC):
            xp2 = dcx % 2
            pp.dma("sync", f"x2f{xp2}", key=f"k_x2f_{dcx}",
                   after=(f"k_xn2_{dcx - 2}",) if dcx >= 2 else (),
                   fn=lambda e, dcx=dcx, xp2=xp2: e.dma_start(
                       out=x2f[xp2][:], in_=x2_scr[dcx]))
            pp.dve(lambda e, xp2=xp2: e.tensor_mul(silu[:], x2f[xp2][:], S2b[:]),
                   key=f"k_xn2m_{dcx}", after=(f"k_x2f_{dcx}", "k_b2se_3") +
                   ((f"k_xn2_{dcx - 1}",) if dcx >= 1 else ()))
            pp.dve(lambda e, dcx=dcx: e.tensor_sub(
                xn2[:, OWN * dcx:OWN * (dcx + 1)], silu[:], M2b[:]),
                key=f"k_xn2_{dcx}", after=(f"k_xn2m_{dcx}", "k_b2me_3"))
        for hc in range(HC):
            wpar = hc % 2
            wdep = (f"k_up1_{hc - 2}",) if hc >= 2 else ()
            pp.dma("pool", f"w1s{wpar}", key=f"k_w1L_{hc}", after=wdep,
                   fn=lambda e, hc=hc, wpar=wpar: e.dma_start(
                       out=w1s[wpar][:].rearrange("p (c f) -> p c f", c=DC),
                       in_=w1t[:, 128 * hc:128 * (hc + 1)].rearrange("(c p) f -> p c f", p=128)))
            pp.dma("pool", f"w3s{wpar}", key=f"k_w3L_{hc}", after=wdep,
                   fn=lambda e, hc=hc, wpar=wpar: e.dma_start(
                       out=w3s[wpar][:].rearrange("p (c f) -> p c f", c=DC),
                       in_=w3t[:, 128 * hc:128 * (hc + 1)].rearrange("(c p) f -> p c f", p=128)))
            for dcx in range(DC):
                dep1 = (f"k_xn2_{dcx}",)
                if dcx == 0:
                    dep1 = dep1 + (f"k_w1L_{hc}",)
                    if hc >= 2:
                        dep1 = dep1 + (f"k_sil_{hc - 2}_{3}",)
                for cb in range(4):
                    pp.mm(lambda e, dcx=dcx, cb=cb, wpar=wpar: e.matmul(
                        ps[cb][:], w1s[wpar][:, 128 * dcx:128 * (dcx + 1)],
                        xn2[:, OWN * dcx + 512 * cb:OWN * dcx + 512 * (cb + 1)],
                        start=(dcx == 0), stop=(dcx == DC - 1)),
                        key=f"k_up1_{hc}" if (dcx == DC - 1 and cb == 3) else None,
                        after=dep1 if cb == 0 else ())
                dep3 = (f"k_xn2_{dcx}",)
                if dcx == 0:
                    dep3 = dep3 + (f"k_w3L_{hc}",)
                    if hc >= 2:
                        dep3 = dep3 + (f"k_hid_{hc - 2}_{3}",)
                for cb in range(4):
                    pp.mm(lambda e, dcx=dcx, cb=cb, wpar=wpar: e.matmul(
                        ps[4 + cb][:], w3s[wpar][:, 128 * dcx:128 * (dcx + 1)],
                        xn2[:, OWN * dcx + 512 * cb:OWN * dcx + 512 * (cb + 1)],
                        start=(dcx == 0), stop=(dcx == DC - 1)),
                        key=f"k_up3_{hc}" if (dcx == DC - 1 and cb == 3) else None,
                        after=dep3 if cb == 0 else ())
            for cb in range(4):
                pp.act(lambda e, cb=cb: e.activation(
                    silu[:, 512 * cb:512 * (cb + 1)], ps[cb][:], AF.Silu),
                    key=f"k_sil_{hc}_{cb}", after=(f"k_up1_{hc}",) +
                    ((f"k_hid_{hc - 1}_{3}",) if hc >= 1 else ()))
                pp.dve(lambda e, cb=cb: e.tensor_mul(
                    hidsb[:, 512 * cb:512 * (cb + 1)], ps[4 + cb][:],
                    silu[:, 512 * cb:512 * (cb + 1)]),
                    key=f"k_hid_{hc}_{cb}",
                    after=(f"k_sil_{hc}_{cb}", f"k_up3_{hc}") +
                    ((f"k_hw_{hc - 1}",) if cb == 0 and hc >= 1 else ()))
            pp.dma("pool", "hw", key=f"k_hw_{hc}",
                   after=tuple(f"k_hid_{hc}_{cb}" for cb in range(4)),
                   fn=lambda e, hc=hc: e.dma_start(out=hid_scr[hc], in_=hidsb[:]))
        pp.barrier()

        for tq in range(4):
            pp.dma("sync", "hq", key=f"k_hq_{tq}",
                   after=(f"k_ya_{tq - 1}_{7}_{1}",) if tq >= 1 else (),
                   fn=lambda e, tq=tq: e.dma_start(
                       out=hidq[:].rearrange("p (c t) -> p c t", c=HC),
                       in_=hid_scr[:, :, 512 * tq:512 * (tq + 1)].rearrange("c p t -> p c t")))
            for dp in range(8):
                wpar = dp % 2
                pbank = [psA, psB] if dp % 2 == 0 else [psC, psD]
                wdep = (f"k_dn_{tq}_{dp - 2}_{1}",) if dp >= 2 else \
                    ((f"k_dn_{tq - 1}_{6 + dp}_{1}",) if tq >= 1 else ())
                pp.dma("pool", f"w2s{wpar}", key=f"k_w2L_{tq}_{dp}", after=wdep,
                       fn=lambda e, dp=dp, wpar=wpar: e.dma_start(
                           out=w2hs[wpar][:].rearrange("p (c w) -> p c w", c=HC),
                           in_=w2h_scr[:, :, dp, :].rearrange("c p w -> p c w")))
                for hc in range(HC):
                    dep0 = ()
                    if hc == 0:
                        dep0 = (f"k_w2L_{tq}_{dp}", f"k_hq_{tq}")
                        if dp >= 2:
                            dep0 = dep0 + (f"k_ya_{tq}_{dp - 2}_{1}",)
                        elif tq >= 1:
                            dep0 = dep0 + (f"k_ya_{tq - 1}_{6 + dp}_{1}",)
                    for j in range(2):
                        pp.mm(lambda e, hc=hc, j=j, wpar=wpar, pbank=pbank: e.matmul(
                            pbank[j][:],
                            w2hs[wpar][:, 256 * hc + 128 * j:256 * hc + 128 * (j + 1)],
                            hidq[:, 512 * hc:512 * (hc + 1)],
                            start=(hc == 0), stop=(hc == HC - 1)),
                            key=f"k_dn_{tq}_{dp}_{j}" if hc == HC - 1 else None,
                            after=dep0 if j == 0 else ())
                for j in range(2):
                    xp = j
                    pp.dma("act", f"x2c{xp}", key=f"k_x2c_{tq}_{dp}_{j}",
                           after=(f"k_ya_{tq}_{dp - 1}_{j}",) if dp >= 1 else
                           ((f"k_ya_{tq - 1}_{7}_{j}",) if tq >= 1 else ()),
                           fn=lambda e, tq=tq, dp=dp, j=j, xp=xp: e.dma_start(
                               out=x2c[xp][:],
                               in_=x2_scr[2 * dp + j][:, 512 * tq:512 * (tq + 1)]))
                    pp.dve(lambda e, j=j, pbank=pbank, xp=xp: e.tensor_add(
                        y_sb[j][:], pbank[j][:], x2c[xp][:]),
                        key=f"k_ya_{tq}_{dp}_{j}",
                        after=(f"k_dn_{tq}_{dp}_{j}", f"k_x2c_{tq}_{dp}_{j}") +
                        ((f"k_yw_{tq}_{dp - 1}_{0}", f"k_yw_{tq}_{dp - 1}_{1}") if dp >= 1 else
                         ((f"k_yw_{tq - 1}_{7}_{0}", f"k_yw_{tq - 1}_{7}_{1}") if tq >= 1 else ())))
                    pp.dma("act", "yw", key=f"k_yw_{tq}_{dp}_{j}",
                           after=(f"k_ya_{tq}_{dp}_{j}",),
                           fn=lambda e, tq=tq, dp=dp, j=j: e.dma_start(
                               out=yT[128 * (2 * dp + j):128 * (2 * dp + j + 1),
                                      512 * tq:512 * (tq + 1)],
                               in_=y_sb[j][:]))
        pp.barrier()

    def emit_rope(pp, tag, deps, out_even, out_odd, use_rout, pb=0):
        rr, rs, t1, t2 = rraw2[pb], rswp2[pb], rt1_2[pb], rt2_2[pb]
        for k4, (a, b2) in enumerate(((0, 32), (32, 0), (64, 96), (96, 64))):
            pp.dma("sync", f"swp{pb}", key=f"k_swp_{tag}_{k4}", after=deps,
                   fn=lambda e, a=a, b2=b2: e.dma_start(
                       out=rs[a:a + 32, :], in_=rr[b2:b2 + 32, :]))
        swpk = tuple(f"k_swp_{tag}_{k4}" for k4 in range(4))
        pp.dve(lambda e: e.tensor_mul(t1[:], rr[:], cosD[:]),
               key=f"k_rp1_{tag}", after=deps)
        pp.dve(lambda e: e.tensor_mul(t2[:], rs[:], sinD[:]),
               key=f"k_rp2_{tag}", after=swpk + deps)
        pp.dve(lambda e: e.tensor_add(out_even, t1[0:64, :], t2[0:64, :]),
               key=f"k_rp3_{tag}", after=(f"k_rp1_{tag}", f"k_rp2_{tag}"))
        pp.dve(lambda e: e.tensor_add(out_odd, t1[64:128, :], t2[64:128, :]),
               key=f"k_rope_{tag}", after=(f"k_rp3_{tag}",))

    with nc.Block() as blk:
        blk.sync(lambda eng: (p.start("sync", eng), walk(p)))
        blk.gpsimd(lambda eng: (p.start("pool", eng), walk(p)))
        blk.tensor(lambda eng: (p.start("pe", eng), walk(p)))
        blk.scalar(lambda eng: (p.start("act", eng), walk(p)))
        blk.vector(lambda eng: (p.start("dve", eng), walk(p)))

    return nc



def _prep(inputs, cfg):
    x = np.ascontiguousarray(inputs["L0_attn_hidden_states"], np.float32)
    B = x.shape[0]
    xT = [np.ascontiguousarray(x[b].T) for b in range(B)]
    wqkv_p = np.ascontiguousarray(inputs["L0_attn_Wqkv"][:, qkv_perm()], np.float32)
    wkup_p = np.ascontiguousarray(inputs["L0_attn_Wk_up"][:, kup_perm()], np.float32)
    wvup = np.ascontiguousarray(inputs["L0_attn_Wv_up"], np.float32)
    wo = np.ascontiguousarray(inputs["L0_attn_Wo"], np.float32)
    w1t = np.ascontiguousarray(inputs["L0_ffn_W1"].T, np.float32)
    w3t = np.ascontiguousarray(inputs["L0_ffn_W3"].T, np.float32)
    w2t = np.ascontiguousarray(inputs["L0_ffn_W2"].T, np.float32)
    in_maps = []
    owns = []
    for c in range(8):
        b = c // 4
        sl = slabs_of(c, cfg)
        ids = np.concatenate([np.arange(512 * s, 512 * (s + 1)) for s in sl])
        owns.append((b, ids))
        in_maps.append({
            "xTb": xT[b],
            "xTo": np.ascontiguousarray(xT[b][:, ids]),
            "posin": ids.astype(np.float32)[None, :],
            "wqkv": wqkv_p, "wkup": wkup_p, "wvup": wvup, "wo": wo,
            "w1t": w1t, "w3t": w3t, "w2t": w2t,
        })
    return in_maps, owns, x.shape


def kernel(**inputs):
    cfg = Cfg(inputs["L0_attn_hidden_states"].shape[1])
    in_maps, owns, xshape = _prep(inputs, cfg)
    nc = build_program(cfg)
    res = run_bass_kernel_spmd(nc, in_maps, core_ids=list(range(8)))
    y = np.zeros(xshape, np.float32)
    for c in range(8):
        b, ids = owns[c]
        y[b, ids, :] = np.asarray(res.results[c]["yT"], np.float32).T
    return y
